# version 06
# baseline (speedup 1.0000x reference)
import math
import numpy as np
import ml_dtypes
from contextlib import ExitStack

import concourse.bass as bass
import concourse.mybir as mybir
from concourse.bass_utils import run_bass_kernel_spmd

F32 = mybir.dt.float32
BF16 = mybir.dt.bfloat16
AF = mybir.ActivationFunctionType
ALU = mybir.AluOpType
AX = mybir.AxisListType

D = 2048
NH = 16
HD = 128
NFC = 16
T_OWN = 2048
TH = 1024
HALO = 512
SCALE = 1.0 / math.sqrt(HD)
NEXP = 16
NIC = 11
NGRP = 8

DEBUG = False

ENGINES = ("sync", "gpsimd", "vector", "scalar", "tensor")
SEMS = ("pe", "act", "vec", "pool", "qw", "qx", "st")


class Em:

    def __init__(self):
        self.plan = {e: [] for e in ENGINES}
        self.meta = {e: [] for e in ENGINES}
        self.cnt = {s: 0 for s in SEMS}
        self.wmax = {}
        self.sems = {}

    def emit(self, eng, fn, meta=("o",)):
        self.plan[eng].append(fn)
        self.meta[eng].append(meta)

    def wait(self, eng, sem, val):
        if val is None or val <= 0:
            return
        assert val <= self.cnt[sem], f"wait on future {sem}: {val} > {self.cnt[sem]}"
        key = (eng, sem)
        if self.wmax.get(key, 0) >= val:
            return
        self.wmax[key] = val
        self.emit(eng, lambda e, s=sem, v=val: e.wait_ge(self.sems[s], v), ("w", sem, val))

    def op(self, eng, build, inc=None, k=1):
        if inc is None:
            self.emit(eng, lambda e, b=build: b(e))
            return None
        self.cnt[inc] += k
        m = self.cnt[inc]
        self.emit(eng, lambda e, b=build, s=inc, kk=k: b(e).then_inc(self.sems[s], kk),
                  ("i", inc, k))
        return m

    def simulate(self):
        pos = {e: 0 for e in ENGINES}
        sem = {x: 0 for x in SEMS}
        progress = True
        while progress:
            progress = False
            for e in ENGINES:
                while pos[e] < len(self.meta[e]):
                    m = self.meta[e][pos[e]]
                    if m[0] == "w" and sem[m[1]] < m[2]:
                        break
                    if m[0] == "i":
                        sem[m[1]] += m[2]
                    pos[e] += 1
                    progress = True
        stuck = {e: (pos[e], len(self.meta[e])) for e in ENGINES if pos[e] < len(self.meta[e])}
        if stuck:
            detail = []
            for e, (p, n) in stuck.items():
                detail.append(f"{e}@{p}/{n} blocked on {self.meta[e][p]}")
            raise RuntimeError("schedule deadlock: " + "; ".join(detail))


class Builder:
    def __init__(self):
        self.nc = bass.Bass("TRN2", target_bir_lowering=False, debug=False)
        self.em = Em()

    def build(self):
        nc = self.nc
        em = self.em
        s = self

        s.dx = nc.dram_tensor("x_own", [T_OWN, D], F32, kind="ExternalInput").ap()
        s.dxh = nc.dram_tensor("x_halo", [HALO, D], F32, kind="ExternalInput").ap()
        s.dmask = nc.dram_tensor("mask_std", [8, 128, 512], BF16, kind="ExternalInput").ap()
        s.dmaskf = nc.dram_tensor("mask_first", [4, 128, 512], BF16, kind="ExternalInput").ap()
        s.dident32 = nc.dram_tensor("ident_f32", [128, 128], F32, kind="ExternalInput").ap()
        s.dident_bf = nc.dram_tensor("ident_bf", [128, 128], BF16, kind="ExternalInput").ap()
        s.dwqk = nc.dram_tensor("wqk_blk", [NGRP, 4, 128, NFC * 128], F32, kind="ExternalInput").ap()
        s.dwv = nc.dram_tensor("wv_blk", [NGRP, 128, NFC * 256], F32, kind="ExternalInput").ap()
        s.dwout = nc.dram_tensor("wout_blk", [NFC, 128, D], F32, kind="ExternalInput").ap()
        s.dgate = nc.dram_tensor("gate_blk", [128, NFC * 16], F32, kind="ExternalInput").ap()
        s.dup = nc.dram_tensor("up_blk", [NEXP, NIC, 128, NFC * 256], F32, kind="ExternalInput").ap()
        s.ddown = nc.dram_tensor("down_blk", [NEXP, 4, 128, NIC * 512], F32, kind="ExternalInput").ap()
        s.dsup = nc.dram_tensor("sup_blk", [2, NIC, 128, NFC * 256], F32, kind="ExternalInput").ap()
        s.dsdown = nc.dram_tensor("sdown_blk", [2, 4, 128, NIC * 512], F32, kind="ExternalInput").ap()
        s.dy = nc.dram_tensor("y", [T_OWN, D], F32, kind="ExternalOutput").ap()
        s.rden_dram = nc.dram_tensor("rden_scr", [1, 512], F32, kind="Internal").ap()
        s.cw_dram = nc.dram_tensor("cw_scr", [1, TH], F32, kind="Internal").ap()
        if DEBUG:
            s.dbg_xn = nc.dram_tensor("dbg_xn", [128, NFC * TH], BF16, kind="ExternalOutput").ap()
            s.dbg_xnh = nc.dram_tensor("dbg_xnh", [128, NFC * HALO], BF16, kind="ExternalOutput").ap()
            s.dbg_q = nc.dram_tensor("dbg_q", [128, 2 * TH], BF16, kind="ExternalOutput").ap()
            s.dbg_k = nc.dram_tensor("dbg_k", [128, 2 * (TH + HALO)], BF16, kind="ExternalOutput").ap()
            s.dbg_v = nc.dram_tensor("dbg_v", [128, 12 * 256], BF16, kind="ExternalOutput").ap()
            s.dbg_p = nc.dram_tensor("dbg_p", [128, 8 * 512], BF16, kind="ExternalOutput").ap()
            s.dbg_oatt = nc.dram_tensor("dbg_oatt", [128, 8 * TH], BF16, kind="ExternalOutput").ap()
            s.dbg_acc = nc.dram_tensor("dbg_acc", [128, 8 * D], F32, kind="ExternalOutput").ap()
            s.dbg_cw = nc.dram_tensor("dbg_cw", [16, TH], F32, kind="ExternalOutput").ap()

        with ExitStack() as ctx:
            E = ctx.enter_context
            s.acc = E(nc.sbuf_tensor("acc", [128, 8 * D], F32))
            s.xn_own = E(nc.sbuf_tensor("xn_own", [128, NFC * TH], BF16))
            s.xn_halo = E(nc.sbuf_tensor("xn_halo", [128, NFC * HALO], BF16))
            s.mask_std = E(nc.sbuf_tensor("mask_std_sb", [128, 8 * 512], BF16))
            s.mask_first = E(nc.sbuf_tensor("mask_first_sb", [128, 4 * 512], BF16))
            s.ident32 = E(nc.sbuf_tensor("ident_f32_sb", [128, 128], F32))
            s.ident_bf = E(nc.sbuf_tensor("ident_bf_sb", [128, 128], BF16))
            s.ones_col = E(nc.sbuf_tensor("ones_col", [128, 1], BF16))
            s.ones128 = E(nc.sbuf_tensor("ones128", [128, 128], BF16))
            s.ones_row = E(nc.sbuf_tensor("ones_row", [1, 128], F32))
            s.gate_sb = E(nc.sbuf_tensor("gate_sb", [128, NFC * 16], BF16))
            s.cw_T = E(nc.sbuf_tensor("cw_T", [16, TH], F32))
            s.ps = [E(nc.psum_tensor(f"ps{i}", [128, 512], F32)) for i in range(6)]
            s.ptr = [E(nc.psum_tensor(f"ptr{i}", [128, 1024], BF16)) for i in range(2)]
            for nm in SEMS:
                em.sems[nm] = E(nc.semaphore(nm))

            s.bank_free = {i: ("vec", 0) for i in range(6)}
            s.ptr_free = {i: ("vec", 0) for i in range(2)}

            self.plan_init()
            for half in (0, 1):
                self.plan_half(half)
            em.wait("sync", "st", em.cnt["st"])

            self.em.simulate()
            self.replay()
        return nc

    def replay(self):
        nc, em = self.nc, self.em
        with nc.Block() as block:
            for eng_name in ENGINES:
                cmds = em.plan[eng_name]

                def runner(e, cmds=cmds):
                    for c in cmds:
                        c(e)
                getattr(block, eng_name)(runner)

    def phase_barrier(self):
        em = self.em
        pe, act, vec = em.cnt["pe"], em.cnt["act"], em.cnt["vec"]
        for q in ("gpsimd", "sync"):
            em.wait(q, "pe", pe)
            em.wait(q, "act", act)
            em.wait(q, "vec", vec)
        em.wait("vector", "pe", pe)
        em.wait("vector", "act", act)
        em.wait("scalar", "pe", pe)
        em.wait("scalar", "vec", vec)
        if DEBUG:
            st = em.cnt["st"]
            for q in ("gpsimd", "sync", "vector", "scalar", "tensor"):
                em.wait(q, "st", st)

    def wait_bank(self, bank):
        sem, val = self.bank_free[bank]
        self.em.wait("tensor", sem, val)

    def use_bank(self, bank, sem, val):
        self.bank_free[bank] = (sem, val)

    def plan_init(self):
        s, em = self, self.em
        s.m_masks = em.op("sync", lambda e: e.dma_start(
            out=s.mask_std[:], in_=s.dmask.rearrange("j p c -> p j c")), inc="qx", k=16)
        s.m_maskf = em.op("sync", lambda e: e.dma_start(
            out=s.mask_first[:], in_=s.dmaskf.rearrange("j p c -> p j c")), inc="qx", k=16)
        s.m_id32 = em.op("sync", lambda e: e.dma_start(out=s.ident32[:], in_=s.dident32), inc="qx", k=16)
        s.m_id_bf = em.op("sync", lambda e: e.dma_start(out=s.ident_bf[:], in_=s.dident_bf), inc="qx", k=16)
        s.m_gate = em.op("gpsimd", lambda e: e.dma_start(out=s.gate_sb[:], in_=s.dgate), inc="qw", k=16)
        s.m_onescol = em.op("vector", lambda e: e.memset(s.ones_col[:], 1.0), inc="vec")
        s.m_ones128 = em.op("vector", lambda e: e.memset(s.ones128[:], 1.0), inc="vec")
        s.m_ones = em.op("vector", lambda e: e.memset(s.ones_row[:], 1.0), inc="vec")

    def plan_half(self, half):
        s, em, nc = self, self.em, self.nc
        s.half = half

        with ExitStack() as c1:
            E = c1.enter_context
            H = f"_h{half}"
            xh_t = E(nc.sbuf_tensor("xh_t" + H, [128, D], F32))
            lnt = E(nc.sbuf_tensor("lnt" + H, [128, 2 * D], BF16))
            st6 = E(nc.sbuf_tensor("st6" + H, [128, 24], F32))
            st2 = E(nc.sbuf_tensor("st2" + H, [128, 2], F32))
            nmean = E(nc.sbuf_tensor("nmean" + H, [128, 1], F32))
            rstd = E(nc.sbuf_tensor("rstd" + H, [128, 1], F32))
            glg = E(nc.sbuf_tensor("glg" + H, [128, 16], F32))
            gmx = E(nc.sbuf_tensor("gmx" + H, [128, 1], F32))
            gex = E(nc.sbuf_tensor("gex" + H, [128, 16], F32))
            gsum = E(nc.sbuf_tensor("gsum" + H, [128, 1], F32))
            gsc = E(nc.sbuf_tensor("gsc" + H, [128, 16], F32))
            ggs = E(nc.sbuf_tensor("ggs" + H, [128, 8], F32))
            gt8a = E(nc.sbuf_tensor("gt8a" + H, [128, 8], F32))
            gem_ = E(nc.sbuf_tensor("gem" + H, [128, 16], F32))
            gmk = E(nc.sbuf_tensor("gmk" + H, [128, 16], F32))
            gt8b = E(nc.sbuf_tensor("gt8b" + H, [128, 8], F32))
            gw = E(nc.sbuf_tensor("gw" + H, [128, 16], F32))
            gws = E(nc.sbuf_tensor("gws" + H, [128, 1], F32))

            s.ggs_init = em.op("vector", lambda e: e.memset(ggs[:, 4:8], -1e30), inc="vec")

            self.plan_ln(half, xh_t, lnt, st6, st2, nmean, rstd)
            self.plan_gate(half, glg, gmx, gex, gsum, gsc, ggs, gt8a, gem_, gmk, gt8b, gw, gws)

        with ExitStack() as c2:
            A = c2.enter_context
            H = f"_h{half}"
            q_buf = A(nc.sbuf_tensor("q_buf" + H, [128, 2 * TH], BF16))
            k_buf = A(nc.sbuf_tensor("k_buf" + H, [128, 2 * (TH + HALO)], BF16))
            v_buf = A(nc.sbuf_tensor("v_buf" + H, [128, 12 * 256], BF16))
            wqk_sb = A(nc.sbuf_tensor("wqk_sb" + H, [128, 2 * NFC * 128], BF16))
            wv_sb = A(nc.sbuf_tensor("wv_sb" + H, [128, NFC * 256], BF16))
            p_buf = A(nc.sbuf_tensor("p_buf" + H, [128, 8 * 512], BF16))
            o_att = A(nc.sbuf_tensor("o_att" + H, [128, 4 * TH], BF16))
            wo_sb = A(nc.sbuf_tensor("wo_sb" + H, [128, 4 * D], BF16))
            rden = A(nc.sbuf_tensor("rden" + H, [1, 512], F32))
            rden2 = A(nc.sbuf_tensor("rden2" + H, [1, 512], F32))
            bc_sb = A(nc.sbuf_tensor("bc_sb" + H, [128, 512], F32))
            self.plan_attn(half, q_buf, k_buf, v_buf, wqk_sb, wv_sb, p_buf, o_att, wo_sb, rden, rden2, bc_sb)

        with ExitStack() as c3:
            M = c3.enter_context
            H = f"_h{half}"
            a_sc = M(nc.sbuf_tensor("a_sc" + H, [128, NIC * TH], BF16))
            up_sb = M(nc.sbuf_tensor("up_sb" + H, [128, 2 * NFC * 256], BF16))
            dn_sb = M(nc.sbuf_tensor("dn_sb" + H, [128, 2 * NIC * 512], BF16))
            cwb_sb = M(nc.sbuf_tensor("cwb_sb" + H, [128, 2 * 512], F32))
            sg_sb = M(nc.sbuf_tensor("sg_sb" + H, [128, 2 * 512], BF16))
            us_sb = M(nc.sbuf_tensor("us_sb" + H, [128, 2 * 512], BF16))
            self.plan_moe(half, a_sc, up_sb, dn_sb, cwb_sb, sg_sb, us_sb)

    def plan_ln(self, half, xh_t, lnt, st6, st2, nmean, rstd):
        s, em = self, self.em
        self.phase_barrier()
        s.xn_mark = getattr(s, "xn_mark", {})
        ptr_i = 0
        lnt_free = [0, 0]
        lnt_free_sem = ["vec", "vec"]
        xh_free = ("vec", 0)

        if half == 1:
            em.wait("vector", "pe", s.half0_end_pe)
            last = None
            for fc in range(NFC):
                last = em.op("vector", lambda e, fc=fc: e.tensor_copy(
                    s.xn_halo[:, fc * HALO:(fc + 1) * HALO],
                    s.xn_own[:, fc * TH + 512:fc * TH + TH]),
                    inc="vec" if fc == NFC - 1 else None)
            s.halo_copy_mark = last

        tiles = ([("halo", i) for i in range(4)] if half == 0 else []) + \
                [("own", i) for i in range(8)]
        for kind, ti in tiles:
            if kind == "halo":
                em.wait("sync", xh_free[0], xh_free[1])
                mx = em.op("sync", lambda e, ti=ti: e.dma_start(
                    out=xh_t[:], in_=s.dxh[ti * 128:(ti + 1) * 128, :]), inc="qx", k=16)
                xsrc = xh_t[:]
            else:
                mx = em.op("sync", lambda e, ti=ti: e.dma_start(
                    out=s.acc[:, ti * D:(ti + 1) * D],
                    in_=s.dx[half * TH + ti * 128:half * TH + (ti + 1) * 128, :]), inc="qx", k=16)
                xsrc = s.acc[:, ti * D:(ti + 1) * D]

            em.wait("vector", "qx", mx)
            last = None
            for i in range(4):
                last = em.op("vector", lambda e, i=i, xsrc=xsrc: e.bn_stats(
                    st6[:, 6 * i:6 * i + 6], xsrc[:, 512 * i:512 * (i + 1)]),
                    inc="vec" if i == 3 else None)
            em.wait("vector", "vec", last)
            m1 = em.op("vector", lambda e: e.bn_aggr(st2[:], st6[:]), inc="vec")
            em.wait("vector", "vec", m1)
            em.op("vector", lambda e: e.tensor_scalar(nmean[:], st2[:, 0:1], -1.0, None, ALU.mult))
            m2 = em.op("vector", lambda e: e.tensor_scalar(rstd[:], st2[:, 1:2], 1e-5, None, ALU.add), inc="vec")
            em.wait("scalar", "vec", m2)
            a1 = em.op("scalar", lambda e: e.activation(rstd[:], rstd[:], AF.Ln), inc="act")
            em.wait("scalar", "act", a1)
            a2 = em.op("scalar", lambda e: e.activation(rstd[:], rstd[:], AF.Exp, scale=-0.5), inc="act")
            em.wait("vector", "act", a2)
            lslot = (ti + (0 if kind == "own" else 8)) % 2
            em.wait("vector", lnt_free_sem[lslot], lnt_free[lslot])
            m3 = em.op("vector", lambda e, xsrc=xsrc, lslot=lslot: e.tensor_scalar(
                lnt[:, lslot * D:(lslot + 1) * D], xsrc, nmean[:], rstd[:], ALU.add, ALU.mult), inc="vec")

            em.wait("tensor", "vec", m3)
            em.wait("tensor", "qx", s.m_id_bf)
            for b in range(4):
                bank = ptr_i % 2
                ptr_i += 1
                sem, val = s.ptr_free[bank]
                em.wait("tensor", sem, val)
                mt = None
                for j in range(4):
                    fc = 4 * b + j
                    mt = em.op("tensor", lambda e, fc=fc, j=j, bank=bank, lslot=lslot: e.transpose(
                        s.ptr[bank][:, j * 128:(j + 1) * 128],
                        lnt[:, lslot * D + fc * 128:lslot * D + (fc + 1) * 128],
                        s.ident_bf[:]),
                        inc="pe" if j == 3 else None)
                em.wait("vector", "pe", mt)
                if kind == "own":
                    dst = s.xn_own[:].rearrange("p (f t) -> p f t", t=TH)[:, 4 * b:4 * b + 4, ti * 128:(ti + 1) * 128]
                else:
                    dst = s.xn_halo[:].rearrange("p (f t) -> p f t", t=HALO)[:, 4 * b:4 * b + 4, ti * 128:(ti + 1) * 128]
                mv = em.op("vector", lambda e, dst=dst, bank=bank: e.tensor_copy(
                    dst, s.ptr[bank][:, 0:512].rearrange("p (f t) -> p f t", t=128)), inc="vec")
                s.ptr_free[bank] = ("vec", mv)
            s.xn_mark[(kind, ti)] = mv
            lnt_free[lslot], lnt_free_sem[lslot] = mt, "pe"
            if kind == "halo":
                xh_free = ("vec", m3)

        s.xn_all_own = max(s.xn_mark[("own", i)] for i in range(8))
        if half == 0:
            s.xn_all_halo = max(s.xn_mark[("halo", i)] for i in range(4))
        else:
            s.xn_all_halo = max(s.halo_copy_mark, 0)
        if DEBUG and half == 0:
            em.wait("sync", "vec", max(s.xn_all_own, s.xn_all_halo))
            em.op("sync", lambda e: e.dma_start(out=s.dbg_xn, in_=s.xn_own[:]), inc="st", k=16)
            em.op("sync", lambda e: e.dma_start(out=s.dbg_xnh, in_=s.xn_halo[:]), inc="st", k=16)

    def plan_gate(self, half, glg, gmx, gex, gsum, gsc, ggs, gt8a, gem_, gmk, gt8b, gw, gws):
        s, em = self, self.em
        B5 = 5

        def edge(m):
            em.wait("vector", "vec", m)

        for tt in range(8):
            em.wait("tensor", "vec", s.xn_mark[("own", tt)])
            em.wait("tensor", "qw", s.m_gate)
            self.wait_bank(B5)
            mg = None
            for kt in range(NFC):
                mg = em.op("tensor", lambda e, kt=kt, tt=tt: e.matmul(
                    s.ps[B5][:, 0:16],
                    s.xn_own[:, kt * TH + tt * 128:kt * TH + (tt + 1) * 128],
                    s.gate_sb[:, kt * 16:(kt + 1) * 16],
                    start=(kt == 0), stop=(kt == NFC - 1)),
                    inc="pe" if kt == NFC - 1 else None)
            em.wait("scalar", "pe", mg)
            ma = em.op("scalar", lambda e: e.copy(glg[:], s.ps[B5][:, 0:16]), inc="act")
            self.use_bank(B5, "act", ma)

            em.wait("vector", "act", ma)
            v1 = em.op("vector", lambda e: e.reduce_max(gmx[:], glg[:], AX.X, negate=True), inc="vec")
            em.wait("scalar", "vec", v1)
            a2 = em.op("scalar", lambda e: e.activation(gex[:], glg[:], AF.Exp, bias=gmx[:], scale=1.0), inc="act")
            em.wait("vector", "act", a2)
            edge(em.op("vector", lambda e: e.reduce_sum(gsum[:], gex[:], AX.X), inc="vec"))
            edge(em.op("vector", lambda e: e.reciprocal(gsum[:], gsum[:]), inc="vec"))
            edge(em.op("vector", lambda e: e.tensor_scalar(gsc[:], gex[:], gsum[:], None, ALU.mult), inc="vec"))
            em.wait("vector", "vec", s.ggs_init)
            edge(em.op("vector", lambda e: e.tensor_reduce(
                ggs[:, 0:4, None], gsc[:].rearrange("p (g e) -> p g e", e=4), AX.X, ALU.max), inc="vec"))
            edge(em.op("vector", lambda e: e.max(out=gt8a[:], in_=ggs[:]), inc="vec"))
            last = None
            for j in range(4):
                last = em.op("vector", lambda e, j=j: e.tensor_scalar(
                    gem_[:, j::4], ggs[:, 0:4], gt8a[:, 1:2], None, ALU.is_ge),
                    inc="vec" if j == 3 else None)
            edge(last)
            edge(em.op("vector", lambda e: e.tensor_mul(gmk[:], gsc[:], gem_[:]), inc="vec"))
            edge(em.op("vector", lambda e: e.max(out=gt8b[:], in_=gmk[:]), inc="vec"))
            edge(em.op("vector", lambda e: e.tensor_scalar(gw[:], gmk[:], gt8b[:, 3:4], None, ALU.is_ge), inc="vec"))
            edge(em.op("vector", lambda e: e.tensor_mul(gw[:], gw[:], gmk[:]), inc="vec"))
            edge(em.op("vector", lambda e: e.reduce_sum(gws[:], gw[:], AX.X), inc="vec"))
            edge(em.op("vector", lambda e: e.tensor_scalar(gws[:], gws[:], 1e-20, None, ALU.add), inc="vec"))
            edge(em.op("vector", lambda e: e.reciprocal(gws[:], gws[:]), inc="vec"))
            v2 = em.op("vector", lambda e: e.tensor_scalar(gw[:], gw[:], gws[:], None, ALU.mult), inc="vec")

            em.wait("tensor", "vec", v2)
            em.wait("tensor", "qx", s.m_id32)
            self.wait_bank(B5)
            mt = em.op("tensor", lambda e: e.transpose(
                s.ps[B5][0:16, 0:128], gw[:], s.ident32[:]), inc="pe")
            em.wait("vector", "pe", mt)
            mv = em.op("vector", lambda e, tt=tt: e.tensor_copy(
                s.cw_T[:, tt * 128:(tt + 1) * 128], s.ps[B5][0:16, 0:128]), inc="vec")
            self.use_bank(B5, "vec", mv)
        s.cw_mark = mv
        if DEBUG and half == 0:
            em.wait("sync", "vec", mv)
            em.op("sync", lambda e: e.dma_start(out=s.dbg_cw, in_=s.cw_T[:]), inc="st", k=16)

    def plan_attn(self, half, q_buf, k_buf, v_buf, wqk_sb, wv_sb, p_buf, o_att, wo_sb, rden, rden2, bc_sb):
        s, em = self, self.em
        self.phase_barrier()

        wqk_slot_free = [0, 0]
        wv_free = 0
        qk_buf_free = 0
        v_buf_free = 0
        o_slot_written = {}
        oproj_end_pe = 0
        unit_idx = 0

        em.wait("tensor", "vec", s.xn_all_own)
        em.wait("tensor", "vec", s.xn_all_halo)

        for g in range(NGRP):
            bank_i = 0
            proj_evacs = []
            for j in range(4):
                slot = j % 2
                em.wait("gpsimd", "pe", wqk_slot_free[slot])
                mload = em.op("gpsimd", lambda e, g=g, j=j, slot=slot: e.dma_start(
                    out=wqk_sb[:, slot * NFC * 128:(slot + 1) * NFC * 128],
                    in_=s.dwqk[g, j]), inc="qw", k=16)
                if j == 0:
                    em.wait("gpsimd", "pe", wv_free)
                    m_wv = em.op("gpsimd", lambda e, g=g: e.dma_start(out=wv_sb[:], in_=s.dwv[g]), inc="qw", k=16)
                hj = j % 2
                is_q = j < 2
                tcs = range(2) if is_q else range(3)
                em.wait("tensor", "qw", mload)
                for tc in tcs:
                    bank = bank_i % 4
                    bank_i += 1
                    self.wait_bank(bank)
                    mm = None
                    for kt in range(NFC):
                        if is_q:
                            rhs = s.xn_own[:, kt * TH + tc * 512:kt * TH + (tc + 1) * 512]
                        elif tc == 0:
                            rhs = s.xn_halo[:, kt * HALO:(kt + 1) * HALO]
                        else:
                            rhs = s.xn_own[:, kt * TH + (tc - 1) * 512:kt * TH + tc * 512]
                        mm = em.op("tensor", lambda e, kt=kt, slot=slot, rhs=rhs, bank=bank: e.matmul(
                            s.ps[bank][:],
                            wqk_sb[:, slot * NFC * 128 + kt * 128:slot * NFC * 128 + (kt + 1) * 128],
                            rhs, start=(kt == 0), stop=(kt == NFC - 1)),
                            inc="pe" if kt == NFC - 1 else None)
                    wqk_slot_free[slot] = mm
                    em.wait("scalar", "pe", mm)
                    em.wait("scalar", "pe", qk_buf_free)
                    if is_q:
                        dst = q_buf[:, hj * TH + tc * 512:hj * TH + (tc + 1) * 512]
                    else:
                        dst = k_buf[:, hj * 1536 + tc * 512:hj * 1536 + (tc + 1) * 512]
                    ma = em.op("scalar", lambda e, dst=dst, bank=bank: e.copy(dst, s.ps[bank][:]), inc="act")
                    self.use_bank(bank, "act", ma)
                    proj_evacs.append(ma)

            em.wait("tensor", "qw", m_wv)
            v_evacs = []
            for tc in range(12):
                bank = bank_i % 4
                bank_i += 1
                self.wait_bank(bank)
                mm = None
                for kt in range(NFC):
                    if tc < 4:
                        lhsT = s.xn_halo[:, kt * HALO + tc * 128:kt * HALO + (tc + 1) * 128]
                    else:
                        lhsT = s.xn_own[:, kt * TH + (tc - 4) * 128:kt * TH + (tc - 3) * 128]
                    mm = em.op("tensor", lambda e, kt=kt, lhsT=lhsT, bank=bank: e.matmul(
                        s.ps[bank][:, 0:256], lhsT, wv_sb[:, kt * 256:(kt + 1) * 256],
                        start=(kt == 0), stop=(kt == NFC - 1)),
                        inc="pe" if kt == NFC - 1 else None)
                wv_free = mm
                em.wait("vector", "pe", mm)
                em.wait("vector", "pe", v_buf_free)
                mv = em.op("vector", lambda e, tc=tc, bank=bank: e.tensor_copy(
                    v_buf[:, tc * 256:(tc + 1) * 256], s.ps[bank][:, 0:256]), inc="vec")
                self.use_bank(bank, "vec", mv)
                v_evacs.append(mv)

            proj_done_act = max(proj_evacs)
            v_done_vec = max(v_evacs)
            if DEBUG and half == 0 and g == 0:
                em.wait("sync", "act", proj_done_act)
                em.wait("sync", "vec", v_done_vec)
                em.op("sync", lambda e, q_buf=q_buf: e.dma_start(out=s.dbg_q, in_=q_buf[:]), inc="st", k=16)
                em.op("sync", lambda e, k_buf=k_buf: e.dma_start(out=s.dbg_k, in_=k_buf[:]), inc="st", k=16)
                em.op("sync", lambda e, v_buf=v_buf: e.dma_start(out=s.dbg_v, in_=v_buf[:]), inc="st", k=16)

            last_av = 0
            for hj in range(2):
                for n in range(2):
                    unit_idx += 1
                    em.wait("tensor", "act", proj_done_act)
                    em.wait("tensor", "vec", v_done_vec)
                    exp_marks = []
                    mask_marks = []
                    den_m = av_m = None
                    for jt in range(8):
                        sbank = jt % 3
                        self.wait_bank(sbank)
                        sm = em.op("tensor", lambda e, hj=hj, n=n, jt=jt, sbank=sbank: e.matmul(
                            s.ps[sbank][:],
                            k_buf[:, hj * 1536 + n * 512 + jt * 128:hj * 1536 + n * 512 + (jt + 1) * 128],
                            q_buf[:, hj * TH + n * 512:hj * TH + (n + 1) * 512],
                            start=True, stop=True), inc="pe")
                        em.wait("scalar", "pe", sm)
                        ea = em.op("scalar", lambda e, jt=jt, sbank=sbank: e.activation(
                            p_buf[:, jt * 512:(jt + 1) * 512], s.ps[sbank][:], AF.Exp, scale=SCALE), inc="act")
                        self.use_bank(sbank, "act", ea)
                        exp_marks.append(ea)
                        em.wait("gpsimd", "act", ea)
                        em.wait("gpsimd", "qx", s.m_maskf)
                        if half == 0 and n == 0 and jt < 4:
                            mask_ap = s.mask_first[:, jt * 512:(jt + 1) * 512]
                        else:
                            mask_ap = s.mask_std[:, jt * 512:(jt + 1) * 512]
                        mk = em.op("gpsimd", lambda e, jt=jt, mask_ap=mask_ap: e.tensor_mul(
                            p_buf[:, jt * 512:(jt + 1) * 512],
                            p_buf[:, jt * 512:(jt + 1) * 512], mask_ap), inc="pool")
                        mask_marks.append(mk)
                        em.wait("tensor", "pool", mk)
                        if jt == 0:
                            self.wait_bank(4)
                            self.wait_bank(3)
                            em.wait("tensor", "vec", s.m_ones128)
                        den_m = em.op("tensor", lambda e, jt=jt: e.matmul(
                            s.ps[4][:], s.ones128[:], p_buf[:, jt * 512:(jt + 1) * 512],
                            start=(jt == 0), stop=(jt == 7)),
                            inc="pe" if jt == 7 else None)
                        av_m = em.op("tensor", lambda e, hj=hj, n=n, jt=jt: e.matmul(
                            s.ps[3][:], v_buf[:, (n * 4 + jt) * 256 + hj * 128:(n * 4 + jt) * 256 + (hj + 1) * 128],
                            p_buf[:, jt * 512:(jt + 1) * 512],
                            start=(jt == 0), stop=(jt == 7)),
                            inc="pe" if jt == 7 else None)
                    last_av = av_m
                    if DEBUG and half == 0 and g == 0 and hj == 0 and n == 0:
                        em.wait("sync", "pool", max(mask_marks))
                        em.op("sync", lambda e, p_buf=p_buf: e.dma_start(out=s.dbg_p, in_=p_buf[:]), inc="st", k=16)
                    em.wait("vector", "pe", den_m)
                    rm = em.op("vector", lambda e: e.reciprocal(rden[:], s.ps[4][0:1, :]), inc="vec")
                    self.use_bank(4, "vec", rm)
                    em.wait("sync", "vec", rm)
                    em.wait("sync", "qw", getattr(s, "rden_read_mark", 0))
                    m_out = em.op("sync", lambda e: e.dma_start(out=s.rden_dram, in_=rden[:]), inc="qx", k=16)
                    em.wait("gpsimd", "qx", m_out)
                    em.wait("gpsimd", "vec", getattr(s, "bc_free", 0))
                    m_bc = em.op("gpsimd", lambda e: e.dma_start(
                        out=bc_sb[:], in_=s.rden_dram.to_broadcast([128, 512])), inc="qw", k=16)
                    s.rden_read_mark = m_bc
                    hslot = (2 * g + hj) % 4
                    em.wait("vector", "pe", av_m)
                    em.wait("vector", "qw", m_bc)
                    em.wait("vector", "pe", oproj_end_pe)
                    ov = em.op("vector", lambda e, hslot=hslot, n=n: e.tensor_tensor(
                        o_att[:, hslot * TH + n * 512:hslot * TH + (n + 1) * 512],
                        s.ps[3][:], bc_sb[:], ALU.mult), inc="vec")
                    self.use_bank(3, "vec", ov)
                    s.bc_free = ov
                    o_slot_written[(hslot, n)] = ov
            qk_buf_free = last_av
            v_buf_free = last_av

            if g % 2 == 1:
                passi = g // 2
                o_ready = max(o_slot_written.values())
                if DEBUG and half == 0 and g == 1:
                    em.wait("sync", "vec", o_ready)
                    em.op("sync", lambda e, o_att=o_att: e.dma_start(out=s.dbg_oatt, in_=o_att[:]), inc="st", k=16)
                wo_loads = []
                for dk in range(4):
                    em.wait("gpsimd", "pe", oproj_end_pe)
                    ml = em.op("gpsimd", lambda e, passi=passi, dk=dk: e.dma_start(
                        out=wo_sb[:, dk * D:(dk + 1) * D],
                        in_=s.dwout[passi * 4 + dk]), inc="qw", k=16)
                    wo_loads.append(ml)
                last_pe = 0
                em.wait("tensor", "vec", o_ready)
                em.wait("tensor", "qw", wo_loads[-1])
                for tt8 in range(8):
                    banks = [(tt8 * 4 + i) % 6 for i in range(4)] if False else [0, 1, 2, 3]
                    for o in range(4):
                        self.wait_bank(banks[o])
                    mmend = None
                    for dk in range(4):
                        for o in range(4):
                            mmend = em.op("tensor", lambda e, dk=dk, tt8=tt8, o=o, b=banks[o]: e.matmul(
                                s.ps[b][:],
                                o_att[:, dk * TH + tt8 * 128:dk * TH + (tt8 + 1) * 128],
                                wo_sb[:, dk * D + o * 512:dk * D + (o + 1) * 512],
                                start=(dk == 0), stop=(dk == 3)),
                                inc="pe" if (dk == 3 and o == 3) else None)
                    last_pe = mmend
                    em.wait("vector", "pe", mmend)
                    for o in range(4):
                        mv = em.op("vector", lambda e, tt8=tt8, o=o, b=banks[o]: e.tensor_add(
                            s.acc[:, tt8 * D + o * 512:tt8 * D + (o + 1) * 512],
                            s.acc[:, tt8 * D + o * 512:tt8 * D + (o + 1) * 512],
                            s.ps[b][:]), inc="vec")
                        self.use_bank(banks[o], "vec", mv)
                oproj_end_pe = last_pe
                if DEBUG and half == 0 and g == 7:
                    em.wait("sync", "vec", em.cnt["vec"])
                    em.op("sync", lambda e: e.dma_start(out=s.dbg_acc, in_=s.acc[:]), inc="st", k=16)
        s.attn_end_pe = em.cnt["pe"]

    def plan_moe(self, half, a_sc, up_sb, dn_sb, cwb_sb, sg_sb, us_sb):
        s, em = self, self.em
        self.phase_barrier()
        up_slot_free = [0, 0]
        dn_slot_free = [0, 0]
        a_tile_mark = {}
        a_sc_free = 0
        sgus_free = [0, 0]
        cwb_free = 0
        store_marks = []

        for ei in range(NEXP + 2):
            is_shared = ei >= NEXP
            part = ei - NEXP
            if not is_shared:
                em.wait("sync", "vec", s.cw_mark)
                em.wait("sync", "qw", getattr(s, "cw_read_mark", 0))
                m_cwo = em.op("sync", lambda e, ei=ei: e.dma_start(
                    out=s.cw_dram, in_=s.cw_T[ei:ei + 1, :]), inc="qx", k=16)
                em.wait("gpsimd", "qx", m_cwo)
                em.wait("gpsimd", "vec", cwb_free)
                cwb_ready = em.op("gpsimd", lambda e: e.dma_start(
                    out=cwb_sb[:], in_=s.cw_dram.to_broadcast([128, TH])), inc="qw", k=16)
                s.cw_read_mark = cwb_ready

            last_umul = 0
            for mc in range(NIC):
                slot = mc % 2
                em.wait("gpsimd", "pe", up_slot_free[slot])
                src = s.dsup[part, mc] if is_shared else s.dup[ei, mc]
                ml = em.op("gpsimd", lambda e, src=src, slot=slot: e.dma_start(
                    out=up_sb[:, slot * NFC * 256:(slot + 1) * NFC * 256], in_=src), inc="qw", k=16)
                em.wait("tensor", "qw", ml)
                for tt in range(2):
                    pslot = (mc * 2 + tt) % 2
                    gbank, ubank = 2 * pslot, 2 * pslot + 1
                    self.wait_bank(gbank)
                    self.wait_bank(ubank)
                    mg = mu = None
                    for kt in range(NFC):
                        base = slot * NFC * 256 + kt * 256
                        rhs = s.xn_own[:, kt * TH + tt * 512:kt * TH + (tt + 1) * 512]
                        mg = em.op("tensor", lambda e, base=base, rhs=rhs, gbank=gbank, kt=kt: e.matmul(
                            s.ps[gbank][:], up_sb[:, base:base + 128], rhs,
                            start=(kt == 0), stop=(kt == NFC - 1)),
                            inc="pe" if kt == NFC - 1 else None)
                        mu = em.op("tensor", lambda e, base=base, rhs=rhs, ubank=ubank, kt=kt: e.matmul(
                            s.ps[ubank][:], up_sb[:, base + 128:base + 256], rhs,
                            start=(kt == 0), stop=(kt == NFC - 1)),
                            inc="pe" if kt == NFC - 1 else None)
                    up_slot_free[slot] = mu
                    em.wait("scalar", "pe", mg)
                    em.wait("scalar", "vec", sgus_free[pslot])
                    msg = em.op("scalar", lambda e, gbank=gbank, pslot=pslot: e.activation(
                        sg_sb[:, pslot * 512:(pslot + 1) * 512], s.ps[gbank][:], AF.Silu), inc="act")
                    self.use_bank(gbank, "act", msg)
                    em.wait("vector", "pe", mu)
                    em.wait("vector", "vec", sgus_free[pslot])
                    if is_shared:
                        mus = em.op("vector", lambda e, ubank=ubank, pslot=pslot: e.tensor_copy(
                            us_sb[:, pslot * 512:(pslot + 1) * 512], s.ps[ubank][:]), inc="vec")
                    else:
                        em.wait("vector", "qw", cwb_ready)
                        mus = em.op("vector", lambda e, ubank=ubank, pslot=pslot, tt=tt: e.tensor_tensor(
                            us_sb[:, pslot * 512:(pslot + 1) * 512], s.ps[ubank][:],
                            cwb_sb[:, tt * 512:(tt + 1) * 512], ALU.mult), inc="vec")
                    self.use_bank(ubank, "vec", mus)
                    last_umul = mus
                    em.wait("vector", "act", msg)
                    em.wait("vector", "vec", mus)
                    em.wait("vector", "pe", a_sc_free)
                    mam = em.op("vector", lambda e, mc=mc, tt=tt, pslot=pslot: e.tensor_mul(
                        a_sc[:, mc * TH + tt * 512:mc * TH + (tt + 1) * 512],
                        sg_sb[:, pslot * 512:(pslot + 1) * 512],
                        us_sb[:, pslot * 512:(pslot + 1) * 512]), inc="vec")
                    sgus_free[pslot] = mam
                    a_tile_mark[(mc, tt)] = mam
            if not is_shared:
                cwb_free = last_umul

            a_ready = [max(a_tile_mark[(k, tt)] for k in range(NIC)) for tt in range(2)]
            last_down_ldw = 0
            for o in range(4):
                slot = o % 2
                em.wait("gpsimd", "pe", dn_slot_free[slot])
                srcd = s.dsdown[part, o] if is_shared else s.ddown[ei, o]
                mld = em.op("gpsimd", lambda e, srcd=srcd, slot=slot: e.dma_start(
                    out=dn_sb[:, slot * NIC * 512:(slot + 1) * NIC * 512], in_=srcd), inc="qw", k=16)
                em.wait("tensor", "qw", mld)
                for tt8 in range(8):
                    bank = 4 + (tt8 % 2)
                    self.wait_bank(bank)
                    em.wait("tensor", "vec", a_ready[tt8 // 4])
                    md = None
                    for k in range(NIC):
                        md = em.op("tensor", lambda e, k=k, tt8=tt8, slot=slot, o=o, bank=bank: e.matmul(
                            s.ps[bank][:],
                            a_sc[:, k * TH + tt8 * 128:k * TH + (tt8 + 1) * 128],
                            dn_sb[:, slot * NIC * 512 + k * 512:slot * NIC * 512 + (k + 1) * 512],
                            start=(k == 0), stop=(k == NIC - 1)),
                            inc="pe" if k == NIC - 1 else None)
                    dn_slot_free[slot] = md
                    last_down_ldw = md
                    em.wait("vector", "pe", md)
                    mv = em.op("vector", lambda e, tt8=tt8, o=o, bank=bank: e.tensor_add(
                        s.acc[:, tt8 * D + o * 512:tt8 * D + (o + 1) * 512],
                        s.acc[:, tt8 * D + o * 512:tt8 * D + (o + 1) * 512],
                        s.ps[bank][:]), inc="vec")
                    self.use_bank(bank, "vec", mv)
                    if is_shared and part == 1:
                        a_tile_mark[("add", o, tt8)] = mv
            a_sc_free = last_down_ldw

        for tt8 in range(8):
            last = max(a_tile_mark[("add", o, tt8)] for o in range(4))
            em.wait("sync", "vec", last)
            ms = em.op("sync", lambda e, tt8=tt8, half=half: e.dma_start(
                out=s.dy[half * TH + tt8 * 128:half * TH + (tt8 + 1) * 128, :],
                in_=s.acc[:, tt8 * D:(tt8 + 1) * D]), inc="st", k=16)
            store_marks.append(ms)
        s.half0_end_pe = em.cnt["pe"]



def _host_layouts(w_qkv, w_out, gate_w, up_w, down_w, sup_w, sdown_w):
    f32 = np.float32
    wqk = np.empty((NGRP, 4, 128, NFC * 128), f32)
    wq = w_qkv[0:D].reshape(NH, 128, NFC, 128)
    wk = w_qkv[D:2 * D].reshape(NH, 128, NFC, 128)
    for g in range(NGRP):
        for j in range(4):
            src = wq if j < 2 else wk
            h = 2 * g + (j % 2)
            wqk[g, j] = src[h].transpose(2, 1, 0).reshape(128, NFC * 128)
    wv = w_qkv[2 * D:3 * D].reshape(NH, 128, NFC, 128)
    wv_blk = np.empty((NGRP, 128, NFC * 256), f32)
    for g in range(NGRP):
        pair = np.stack([wv[2 * g], wv[2 * g + 1]], axis=0)
        wv_blk[g] = pair.transpose(3, 2, 0, 1).reshape(128, NFC * 256)
    wout_blk = np.ascontiguousarray(
        w_out.reshape(D, NFC, 128).transpose(1, 2, 0))
    gate_blk = np.ascontiguousarray(
        gate_w.reshape(16, NFC, 128).transpose(2, 1, 0).reshape(128, NFC * 16))
    up_blk = np.ascontiguousarray(
        up_w.reshape(NEXP, 2, NIC, 128, NFC, 128)
        .transpose(0, 2, 5, 4, 1, 3)
        .reshape(NEXP, NIC, 128, NFC * 256))
    down_blk = np.ascontiguousarray(
        down_w.reshape(NEXP, 4, 512, NIC, 128)
        .transpose(0, 1, 4, 3, 2)
        .reshape(NEXP, 4, 128, NIC * 512))
    sup_blk = np.ascontiguousarray(
        sup_w.reshape(2, 2, NIC, 128, NFC, 128)
        .transpose(1, 2, 5, 4, 0, 3)
        .reshape(2, NIC, 128, NFC * 256))
    sdown_blk = np.ascontiguousarray(
        sdown_w.reshape(4, 512, 2, NIC, 128)
        .transpose(2, 0, 4, 3, 1)
        .reshape(2, 4, 128, NIC * 512))
    return wqk, wv_blk, wout_blk, gate_blk, up_blk, down_blk, sup_blk, sdown_blk


def _masks():
    ka = np.arange(1024)[:, None]
    qa = np.arange(512)[None, :]
    band = ((qa <= ka) & (ka <= qa + 512)).astype(np.float32)
    m = band.reshape(8, 128, 512).astype(ml_dtypes.bfloat16)
    return m


_BUILT = {}


def _get_program():
    if "nc" not in _BUILT:
        b = Builder()
        _BUILT["nc"] = b.build()
    return _BUILT["nc"]


def kernel(L0_attn_x, L0_attn_w_qkv, L0_attn_w_out, L0_ffn_gate_w,
           L0_ffn_expert_up_w, L0_ffn_expert_down_w,
           L0_ffn_shared_up_w, L0_ffn_shared_down_w):
    x = np.asarray(L0_attn_x, np.float32)
    (wqk, wv_blk, wout_blk, gate_blk, up_blk,
     down_blk, sup_blk, sdown_blk) = _host_layouts(
        np.asarray(L0_attn_w_qkv, np.float32),
        np.asarray(L0_attn_w_out, np.float32),
        np.asarray(L0_ffn_gate_w, np.float32),
        np.asarray(L0_ffn_expert_up_w, np.float32),
        np.asarray(L0_ffn_expert_down_w, np.float32),
        np.asarray(L0_ffn_shared_up_w, np.float32),
        np.asarray(L0_ffn_shared_down_w, np.float32))
    mask_std = _masks()
    mask_first = np.ascontiguousarray(mask_std[0:4])
    mask_first0 = np.zeros_like(mask_first)
    ident32 = np.eye(128, dtype=np.float32)
    ident_bf = np.eye(128, dtype=ml_dtypes.bfloat16)

    shared = {
        "mask_std": mask_std, "ident_f32": ident32, "ident_bf": ident_bf,
        "wqk_blk": wqk, "wv_blk": wv_blk, "wout_blk": wout_blk,
        "gate_blk": gate_blk, "up_blk": up_blk, "down_blk": down_blk,
        "sup_blk": sup_blk, "sdown_blk": sdown_blk,
    }
    in_maps = []
    for c in range(8):
        b, qt = c // 4, c % 4
        rows = x[b, qt * T_OWN:(qt + 1) * T_OWN, :]
        if qt == 0:
            halo = np.zeros((HALO, D), np.float32)
            mf = mask_first0
        else:
            halo = x[b, qt * T_OWN - HALO:qt * T_OWN, :]
            mf = mask_first
        im = dict(shared)
        im["x_own"] = np.ascontiguousarray(rows)
        im["x_halo"] = np.ascontiguousarray(halo)
        im["mask_first"] = mf
        in_maps.append(im)

    nc = _get_program()
    res = run_bass_kernel_spmd(nc, in_maps, core_ids=list(range(8)))
    out = np.empty((2, 8192, D), np.float32)
    for c in range(8):
        b, qt = c // 4, c % 4
        out[b, qt * T_OWN:(qt + 1) * T_OWN, :] = res.results[c]["y"]
    if DEBUG:
        global last_debug
        last_debug = res.results
    return out


last_debug = None


# version 07
# speedup vs baseline: 1.0318x; 1.0318x over previous
import math
import numpy as np
import ml_dtypes
from contextlib import ExitStack

import concourse.bass as bass
import concourse.mybir as mybir
from concourse.bass_utils import run_bass_kernel_spmd

F32 = mybir.dt.float32
BF16 = mybir.dt.bfloat16
AF = mybir.ActivationFunctionType
ALU = mybir.AluOpType
AX = mybir.AxisListType

D = 2048
NH = 16
HD = 128
NFC = 16
T_OWN = 2048
TH = 1024
HALO = 512
SCALE = 1.0 / math.sqrt(HD)
NEXP = 16
NIC = 11
NGRP = 8

DEBUG = False

ENGINES = ("sync", "gpsimd", "vector", "scalar", "tensor")
SEMS = ("pe", "act", "vec", "pool", "qw", "qx", "st")


class Em:

    def __init__(self):
        self.plan = {e: [] for e in ENGINES}
        self.meta = {e: [] for e in ENGINES}
        self.cnt = {s: 0 for s in SEMS}
        self.wmax = {}
        self.sems = {}

    def emit(self, eng, fn, meta=("o",)):
        self.plan[eng].append(fn)
        self.meta[eng].append(meta)

    def wait(self, eng, sem, val):
        if val is None or val <= 0:
            return
        assert val <= self.cnt[sem], f"wait on future {sem}: {val} > {self.cnt[sem]}"
        key = (eng, sem)
        if self.wmax.get(key, 0) >= val:
            return
        self.wmax[key] = val
        self.emit(eng, lambda e, s=sem, v=val: e.wait_ge(self.sems[s], v), ("w", sem, val))

    def op(self, eng, build, inc=None, k=1):
        if inc is None:
            self.emit(eng, lambda e, b=build: b(e))
            return None
        self.cnt[inc] += k
        m = self.cnt[inc]
        self.emit(eng, lambda e, b=build, s=inc, kk=k: b(e).then_inc(self.sems[s], kk),
                  ("i", inc, k))
        return m

    def simulate(self):
        pos = {e: 0 for e in ENGINES}
        sem = {x: 0 for x in SEMS}
        progress = True
        while progress:
            progress = False
            for e in ENGINES:
                while pos[e] < len(self.meta[e]):
                    m = self.meta[e][pos[e]]
                    if m[0] == "w" and sem[m[1]] < m[2]:
                        break
                    if m[0] == "i":
                        sem[m[1]] += m[2]
                    pos[e] += 1
                    progress = True
        stuck = {e: (pos[e], len(self.meta[e])) for e in ENGINES if pos[e] < len(self.meta[e])}
        if stuck:
            detail = []
            for e, (p, n) in stuck.items():
                detail.append(f"{e}@{p}/{n} blocked on {self.meta[e][p]}")
            raise RuntimeError("schedule deadlock: " + "; ".join(detail))


class Builder:
    def __init__(self):
        self.nc = bass.Bass("TRN2", target_bir_lowering=False, debug=False)
        self.em = Em()

    def build(self):
        nc = self.nc
        em = self.em
        s = self

        s.dx = nc.dram_tensor("x_own", [T_OWN, D], F32, kind="ExternalInput").ap()
        s.dxh = nc.dram_tensor("x_halo", [HALO, D], F32, kind="ExternalInput").ap()
        s.dmask = nc.dram_tensor("mask_std", [8, 128, 512], BF16, kind="ExternalInput").ap()
        s.dmaskf = nc.dram_tensor("mask_first", [4, 128, 512], BF16, kind="ExternalInput").ap()
        s.dident32 = nc.dram_tensor("ident_f32", [128, 128], F32, kind="ExternalInput").ap()
        s.dident_bf = nc.dram_tensor("ident_bf", [128, 128], BF16, kind="ExternalInput").ap()
        s.dwqk = nc.dram_tensor("wqk_blk", [NGRP, 4, 128, NFC * 128], F32, kind="ExternalInput").ap()
        s.dwv = nc.dram_tensor("wv_blk", [NGRP, 128, NFC * 256], F32, kind="ExternalInput").ap()
        s.dwout = nc.dram_tensor("wout_blk", [NFC, 128, D], F32, kind="ExternalInput").ap()
        s.dgate = nc.dram_tensor("gate_blk", [128, NFC * 16], F32, kind="ExternalInput").ap()
        s.dup = nc.dram_tensor("up_blk", [NEXP, NIC, 128, NFC * 256], F32, kind="ExternalInput").ap()
        s.ddown = nc.dram_tensor("down_blk", [NEXP, 4, 128, NIC * 512], F32, kind="ExternalInput").ap()
        s.dsup = nc.dram_tensor("sup_blk", [2, NIC, 128, NFC * 256], F32, kind="ExternalInput").ap()
        s.dsdown = nc.dram_tensor("sdown_blk", [2, 4, 128, NIC * 512], F32, kind="ExternalInput").ap()
        s.dy = nc.dram_tensor("y", [T_OWN, D], F32, kind="ExternalOutput").ap()
        s.rden_dram = nc.dram_tensor("rden_scr", [1, 512], F32, kind="Internal").ap()
        s.cw_dram = nc.dram_tensor("cw_scr", [1, TH], F32, kind="Internal").ap()
        if DEBUG:
            s.dbg_xn = nc.dram_tensor("dbg_xn", [128, NFC * TH], BF16, kind="ExternalOutput").ap()
            s.dbg_xnh = nc.dram_tensor("dbg_xnh", [128, NFC * HALO], BF16, kind="ExternalOutput").ap()
            s.dbg_q = nc.dram_tensor("dbg_q", [128, 2 * TH], BF16, kind="ExternalOutput").ap()
            s.dbg_k = nc.dram_tensor("dbg_k", [128, 2 * (TH + HALO)], BF16, kind="ExternalOutput").ap()
            s.dbg_v = nc.dram_tensor("dbg_v", [128, 12 * 256], BF16, kind="ExternalOutput").ap()
            s.dbg_p = nc.dram_tensor("dbg_p", [128, 8 * 512], BF16, kind="ExternalOutput").ap()
            s.dbg_oatt = nc.dram_tensor("dbg_oatt", [128, 8 * TH], BF16, kind="ExternalOutput").ap()
            s.dbg_acc = nc.dram_tensor("dbg_acc", [128, 8 * D], F32, kind="ExternalOutput").ap()
            s.dbg_cw = nc.dram_tensor("dbg_cw", [16, TH], F32, kind="ExternalOutput").ap()

        with ExitStack() as ctx:
            E = ctx.enter_context
            s.acc = E(nc.sbuf_tensor("acc", [128, 8 * D], F32))
            s.xn_own = E(nc.sbuf_tensor("xn_own", [128, NFC * TH], BF16))
            s.xn_halo = E(nc.sbuf_tensor("xn_halo", [128, NFC * HALO], BF16))
            s.mask_std = E(nc.sbuf_tensor("mask_std_sb", [128, 8 * 512], BF16))
            s.mask_first = E(nc.sbuf_tensor("mask_first_sb", [128, 4 * 512], BF16))
            s.ident32 = E(nc.sbuf_tensor("ident_f32_sb", [128, 128], F32))
            s.ident_bf = E(nc.sbuf_tensor("ident_bf_sb", [128, 128], BF16))
            s.ones_col = E(nc.sbuf_tensor("ones_col", [128, 1], BF16))
            s.ones128 = E(nc.sbuf_tensor("ones128", [128, 128], BF16))
            s.ones_row = E(nc.sbuf_tensor("ones_row", [1, 128], F32))
            s.gate_sb = E(nc.sbuf_tensor("gate_sb", [128, NFC * 16], BF16))
            s.cw_T = E(nc.sbuf_tensor("cw_T", [16, TH], F32))
            s.ps = [E(nc.psum_tensor(f"ps{i}", [128, 512], F32)) for i in range(6)]
            s.ptr = [E(nc.psum_tensor(f"ptr{i}", [128, 1024], BF16)) for i in range(2)]
            for nm in SEMS:
                em.sems[nm] = E(nc.semaphore(nm))

            s.bank_free = {i: ("vec", 0) for i in range(6)}
            s.ptr_free = {i: ("vec", 0) for i in range(2)}

            self.plan_init()
            for half in (0, 1):
                self.plan_half(half)
            em.wait("sync", "st", em.cnt["st"])

            self.em.simulate()
            self.replay()
        return nc

    def replay(self):
        nc, em = self.nc, self.em
        with nc.Block() as block:
            for eng_name in ENGINES:
                cmds = em.plan[eng_name]

                def runner(e, cmds=cmds):
                    for c in cmds:
                        c(e)
                getattr(block, eng_name)(runner)

    def phase_barrier(self):
        em = self.em
        pe, act, vec = em.cnt["pe"], em.cnt["act"], em.cnt["vec"]
        for q in ("gpsimd", "sync"):
            em.wait(q, "pe", pe)
            em.wait(q, "act", act)
            em.wait(q, "vec", vec)
        em.wait("vector", "pe", pe)
        em.wait("vector", "act", act)
        em.wait("scalar", "pe", pe)
        em.wait("scalar", "vec", vec)
        if DEBUG:
            st = em.cnt["st"]
            for q in ("gpsimd", "sync", "vector", "scalar", "tensor"):
                em.wait(q, "st", st)

    def wait_bank(self, bank):
        sem, val = self.bank_free[bank]
        self.em.wait("tensor", sem, val)

    def use_bank(self, bank, sem, val):
        self.bank_free[bank] = (sem, val)

    def plan_init(self):
        s, em = self, self.em
        s.m_masks = em.op("sync", lambda e: e.dma_start(
            out=s.mask_std[:], in_=s.dmask.rearrange("j p c -> p j c")), inc="qx", k=16)
        s.m_maskf = em.op("sync", lambda e: e.dma_start(
            out=s.mask_first[:], in_=s.dmaskf.rearrange("j p c -> p j c")), inc="qx", k=16)
        s.m_id32 = em.op("sync", lambda e: e.dma_start(out=s.ident32[:], in_=s.dident32), inc="qx", k=16)
        s.m_id_bf = em.op("sync", lambda e: e.dma_start(out=s.ident_bf[:], in_=s.dident_bf), inc="qx", k=16)
        s.m_gate = em.op("gpsimd", lambda e: e.dma_start(out=s.gate_sb[:], in_=s.dgate), inc="qw", k=16)
        s.m_onescol = em.op("vector", lambda e: e.memset(s.ones_col[:], 1.0), inc="vec")
        s.m_ones128 = em.op("vector", lambda e: e.memset(s.ones128[:], 1.0), inc="vec")
        s.m_ones = em.op("vector", lambda e: e.memset(s.ones_row[:], 1.0), inc="vec")

    def plan_half(self, half):
        s, em, nc = self, self.em, self.nc
        s.half = half

        with ExitStack() as c1:
            E = c1.enter_context
            H = f"_h{half}"
            xh_t = E(nc.sbuf_tensor("xh_t" + H, [128, D], F32))
            lnt = E(nc.sbuf_tensor("lnt" + H, [128, 2 * D], BF16))
            st6 = E(nc.sbuf_tensor("st6" + H, [128, 24], F32))
            st2 = E(nc.sbuf_tensor("st2" + H, [128, 2], F32))
            nmean = E(nc.sbuf_tensor("nmean" + H, [128, 1], F32))
            rstd = E(nc.sbuf_tensor("rstd" + H, [128, 1], F32))
            glg = E(nc.sbuf_tensor("glg" + H, [128, 16], F32))
            gmx = E(nc.sbuf_tensor("gmx" + H, [128, 1], F32))
            gex = E(nc.sbuf_tensor("gex" + H, [128, 16], F32))
            gsum = E(nc.sbuf_tensor("gsum" + H, [128, 1], F32))
            gsc = E(nc.sbuf_tensor("gsc" + H, [128, 16], F32))
            ggs = E(nc.sbuf_tensor("ggs" + H, [128, 8], F32))
            gt8a = E(nc.sbuf_tensor("gt8a" + H, [128, 8], F32))
            gem_ = E(nc.sbuf_tensor("gem" + H, [128, 16], F32))
            gmk = E(nc.sbuf_tensor("gmk" + H, [128, 16], F32))
            gt8b = E(nc.sbuf_tensor("gt8b" + H, [128, 8], F32))
            gw = E(nc.sbuf_tensor("gw" + H, [128, 16], F32))
            gws = E(nc.sbuf_tensor("gws" + H, [128, 1], F32))

            s.ggs_init = em.op("vector", lambda e: e.memset(ggs[:, 4:8], -1e30), inc="vec")

            self.plan_ln(half, xh_t, lnt, st6, st2, nmean, rstd)
            self.plan_gate(half, glg, gmx, gex, gsum, gsc, ggs, gt8a, gem_, gmk, gt8b, gw, gws)

        with ExitStack() as c2:
            A = c2.enter_context
            H = f"_h{half}"
            q_buf = A(nc.sbuf_tensor("q_buf" + H, [128, 2 * TH], BF16))
            k_buf = A(nc.sbuf_tensor("k_buf" + H, [128, 2 * (TH + HALO)], BF16))
            v_buf = A(nc.sbuf_tensor("v_buf" + H, [128, 12 * 256], BF16))
            wqk_sb = A(nc.sbuf_tensor("wqk_sb" + H, [128, 2 * NFC * 128], BF16))
            wv_sb = A(nc.sbuf_tensor("wv_sb" + H, [128, NFC * 256], BF16))
            p_buf = A(nc.sbuf_tensor("p_buf" + H, [128, 8 * 512], BF16))
            o_att = A(nc.sbuf_tensor("o_att" + H, [128, 4 * TH], BF16))
            wo_sb = A(nc.sbuf_tensor("wo_sb" + H, [128, 4 * D], BF16))
            rden = A(nc.sbuf_tensor("rden" + H, [1, 512], F32))
            rden2 = A(nc.sbuf_tensor("rden2" + H, [1, 512], F32))
            bc_sb = A(nc.sbuf_tensor("bc_sb" + H, [128, 512], F32))
            self.plan_attn(half, q_buf, k_buf, v_buf, wqk_sb, wv_sb, p_buf, o_att, wo_sb, rden, rden2, bc_sb)

        with ExitStack() as c3:
            M = c3.enter_context
            H = f"_h{half}"
            a_sc = M(nc.sbuf_tensor("a_sc" + H, [128, NIC * TH], BF16))
            up_sb = M(nc.sbuf_tensor("up_sb" + H, [128, 2 * NFC * 256], BF16))
            dn_sb = M(nc.sbuf_tensor("dn_sb" + H, [128, 2 * NIC * 512], BF16))
            cwb_sb = M(nc.sbuf_tensor("cwb_sb" + H, [128, 2 * 512], F32))
            sg_sb = M(nc.sbuf_tensor("sg_sb" + H, [128, 2 * 512], BF16))
            us_sb = M(nc.sbuf_tensor("us_sb" + H, [128, 2 * 512], BF16))
            self.plan_moe(half, a_sc, up_sb, dn_sb, cwb_sb, sg_sb, us_sb)

    def plan_ln(self, half, xh_t, lnt, st6, st2, nmean, rstd):
        s, em = self, self.em
        self.phase_barrier()
        s.xn_mark = getattr(s, "xn_mark", {})
        ptr_i = 0
        lnt_free = [0, 0]
        lnt_free_sem = ["vec", "vec"]
        xh_free = ("vec", 0)

        if half == 1:
            em.wait("vector", "pe", s.half0_end_pe)
            last = None
            for fc in range(NFC):
                last = em.op("vector", lambda e, fc=fc: e.tensor_copy(
                    s.xn_halo[:, fc * HALO:(fc + 1) * HALO],
                    s.xn_own[:, fc * TH + 512:fc * TH + TH]),
                    inc="vec" if fc == NFC - 1 else None)
            s.halo_copy_mark = last

        tiles = ([("halo", i) for i in range(4)] if half == 0 else []) + \
                [("own", i) for i in range(8)]
        for kind, ti in tiles:
            if kind == "halo":
                em.wait("sync", xh_free[0], xh_free[1])
                mx = em.op("sync", lambda e, ti=ti: e.dma_start(
                    out=xh_t[:], in_=s.dxh[ti * 128:(ti + 1) * 128, :]), inc="qx", k=16)
                xsrc = xh_t[:]
            else:
                mx = em.op("sync", lambda e, ti=ti: e.dma_start(
                    out=s.acc[:, ti * D:(ti + 1) * D],
                    in_=s.dx[half * TH + ti * 128:half * TH + (ti + 1) * 128, :]), inc="qx", k=16)
                xsrc = s.acc[:, ti * D:(ti + 1) * D]

            em.wait("vector", "qx", mx)
            last = None
            for i in range(4):
                last = em.op("vector", lambda e, i=i, xsrc=xsrc: e.bn_stats(
                    st6[:, 6 * i:6 * i + 6], xsrc[:, 512 * i:512 * (i + 1)]),
                    inc="vec" if i == 3 else None)
            em.wait("vector", "vec", last)
            m1 = em.op("vector", lambda e: e.bn_aggr(st2[:], st6[:]), inc="vec")
            em.wait("vector", "vec", m1)
            em.op("vector", lambda e: e.tensor_scalar(nmean[:], st2[:, 0:1], -1.0, None, ALU.mult))
            m2 = em.op("vector", lambda e: e.tensor_scalar(rstd[:], st2[:, 1:2], 1e-5, None, ALU.add), inc="vec")
            em.wait("scalar", "vec", m2)
            a1 = em.op("scalar", lambda e: e.activation(rstd[:], rstd[:], AF.Ln), inc="act")
            em.wait("scalar", "act", a1)
            a2 = em.op("scalar", lambda e: e.activation(rstd[:], rstd[:], AF.Exp, scale=-0.5), inc="act")
            em.wait("vector", "act", a2)
            lslot = (ti + (0 if kind == "own" else 8)) % 2
            em.wait("vector", lnt_free_sem[lslot], lnt_free[lslot])
            m3 = em.op("vector", lambda e, xsrc=xsrc, lslot=lslot: e.tensor_scalar(
                lnt[:, lslot * D:(lslot + 1) * D], xsrc, nmean[:], rstd[:], ALU.add, ALU.mult), inc="vec")

            em.wait("tensor", "vec", m3)
            em.wait("tensor", "qx", s.m_id_bf)
            for b in range(4):
                bank = ptr_i % 2
                ptr_i += 1
                sem, val = s.ptr_free[bank]
                em.wait("tensor", sem, val)
                mt = None
                for j in range(4):
                    fc = 4 * b + j
                    mt = em.op("tensor", lambda e, fc=fc, j=j, bank=bank, lslot=lslot: e.transpose(
                        s.ptr[bank][:, j * 128:(j + 1) * 128],
                        lnt[:, lslot * D + fc * 128:lslot * D + (fc + 1) * 128],
                        s.ident_bf[:]),
                        inc="pe" if j == 3 else None)
                em.wait("vector", "pe", mt)
                if kind == "own":
                    dst = s.xn_own[:].rearrange("p (f t) -> p f t", t=TH)[:, 4 * b:4 * b + 4, ti * 128:(ti + 1) * 128]
                else:
                    dst = s.xn_halo[:].rearrange("p (f t) -> p f t", t=HALO)[:, 4 * b:4 * b + 4, ti * 128:(ti + 1) * 128]
                mv = em.op("vector", lambda e, dst=dst, bank=bank: e.tensor_copy(
                    dst, s.ptr[bank][:, 0:512].rearrange("p (f t) -> p f t", t=128)), inc="vec")
                s.ptr_free[bank] = ("vec", mv)
            s.xn_mark[(kind, ti)] = mv
            lnt_free[lslot], lnt_free_sem[lslot] = mt, "pe"
            if kind == "halo":
                xh_free = ("vec", m3)

        s.xn_all_own = max(s.xn_mark[("own", i)] for i in range(8))
        if half == 0:
            s.xn_all_halo = max(s.xn_mark[("halo", i)] for i in range(4))
        else:
            s.xn_all_halo = max(s.halo_copy_mark, 0)
        if DEBUG and half == 0:
            em.wait("sync", "vec", max(s.xn_all_own, s.xn_all_halo))
            em.op("sync", lambda e: e.dma_start(out=s.dbg_xn, in_=s.xn_own[:]), inc="st", k=16)
            em.op("sync", lambda e: e.dma_start(out=s.dbg_xnh, in_=s.xn_halo[:]), inc="st", k=16)

    def plan_gate(self, half, glg, gmx, gex, gsum, gsc, ggs, gt8a, gem_, gmk, gt8b, gw, gws):
        s, em = self, self.em
        B5 = 5

        def edge(m):
            em.wait("vector", "vec", m)

        for tt in range(8):
            em.wait("tensor", "vec", s.xn_mark[("own", tt)])
            em.wait("tensor", "qw", s.m_gate)
            self.wait_bank(B5)
            mg = None
            for kt in range(NFC):
                mg = em.op("tensor", lambda e, kt=kt, tt=tt: e.matmul(
                    s.ps[B5][:, 0:16],
                    s.xn_own[:, kt * TH + tt * 128:kt * TH + (tt + 1) * 128],
                    s.gate_sb[:, kt * 16:(kt + 1) * 16],
                    start=(kt == 0), stop=(kt == NFC - 1)),
                    inc="pe" if kt == NFC - 1 else None)
            em.wait("scalar", "pe", mg)
            ma = em.op("scalar", lambda e: e.copy(glg[:], s.ps[B5][:, 0:16]), inc="act")
            self.use_bank(B5, "act", ma)

            em.wait("vector", "act", ma)
            v1 = em.op("vector", lambda e: e.reduce_max(gmx[:], glg[:], AX.X, negate=True), inc="vec")
            em.wait("scalar", "vec", v1)
            a2 = em.op("scalar", lambda e: e.activation(gex[:], glg[:], AF.Exp, bias=gmx[:], scale=1.0), inc="act")
            em.wait("vector", "act", a2)
            edge(em.op("vector", lambda e: e.reduce_sum(gsum[:], gex[:], AX.X), inc="vec"))
            edge(em.op("vector", lambda e: e.reciprocal(gsum[:], gsum[:]), inc="vec"))
            edge(em.op("vector", lambda e: e.tensor_scalar(gsc[:], gex[:], gsum[:], None, ALU.mult), inc="vec"))
            em.wait("vector", "vec", s.ggs_init)
            edge(em.op("vector", lambda e: e.tensor_reduce(
                ggs[:, 0:4, None], gsc[:].rearrange("p (g e) -> p g e", e=4), AX.X, ALU.max), inc="vec"))
            edge(em.op("vector", lambda e: e.max(out=gt8a[:], in_=ggs[:]), inc="vec"))
            last = None
            for j in range(4):
                last = em.op("vector", lambda e, j=j: e.tensor_scalar(
                    gem_[:, j::4], ggs[:, 0:4], gt8a[:, 1:2], None, ALU.is_ge),
                    inc="vec" if j == 3 else None)
            edge(last)
            edge(em.op("vector", lambda e: e.tensor_mul(gmk[:], gsc[:], gem_[:]), inc="vec"))
            edge(em.op("vector", lambda e: e.max(out=gt8b[:], in_=gmk[:]), inc="vec"))
            edge(em.op("vector", lambda e: e.tensor_scalar(gw[:], gmk[:], gt8b[:, 3:4], None, ALU.is_ge), inc="vec"))
            edge(em.op("vector", lambda e: e.tensor_mul(gw[:], gw[:], gmk[:]), inc="vec"))
            edge(em.op("vector", lambda e: e.reduce_sum(gws[:], gw[:], AX.X), inc="vec"))
            edge(em.op("vector", lambda e: e.tensor_scalar(gws[:], gws[:], 1e-20, None, ALU.add), inc="vec"))
            edge(em.op("vector", lambda e: e.reciprocal(gws[:], gws[:]), inc="vec"))
            v2 = em.op("vector", lambda e: e.tensor_scalar(gw[:], gw[:], gws[:], None, ALU.mult), inc="vec")

            em.wait("tensor", "vec", v2)
            em.wait("tensor", "qx", s.m_id32)
            self.wait_bank(B5)
            mt = em.op("tensor", lambda e: e.transpose(
                s.ps[B5][0:16, 0:128], gw[:], s.ident32[:]), inc="pe")
            em.wait("vector", "pe", mt)
            mv = em.op("vector", lambda e, tt=tt: e.tensor_copy(
                s.cw_T[:, tt * 128:(tt + 1) * 128], s.ps[B5][0:16, 0:128]), inc="vec")
            self.use_bank(B5, "vec", mv)
        s.cw_mark = mv
        if DEBUG and half == 0:
            em.wait("sync", "vec", mv)
            em.op("sync", lambda e: e.dma_start(out=s.dbg_cw, in_=s.cw_T[:]), inc="st", k=16)

    def plan_attn(self, half, q_buf, k_buf, v_buf, wqk_sb, wv_sb, p_buf, o_att, wo_sb, rden, rden2, bc_sb):
        s, em = self, self.em
        self.phase_barrier()

        wqk_slot_free = [0, 0]
        wv_free = 0
        qk_buf_free = 0
        v_buf_free = 0
        o_slot_written = {}
        oproj_end_pe = 0
        unit_idx = 0

        em.wait("tensor", "vec", s.xn_all_own)
        em.wait("tensor", "vec", s.xn_all_halo)

        for g in range(NGRP):
            bank_i = 0
            proj_evacs = []
            for j in range(4):
                slot = j % 2
                em.wait("gpsimd", "pe", wqk_slot_free[slot])
                mload = em.op("gpsimd", lambda e, g=g, j=j, slot=slot: e.dma_start(
                    out=wqk_sb[:, slot * NFC * 128:(slot + 1) * NFC * 128],
                    in_=s.dwqk[g, j]), inc="qw", k=16)
                if j == 0:
                    em.wait("gpsimd", "pe", wv_free)
                    m_wv = em.op("gpsimd", lambda e, g=g: e.dma_start(out=wv_sb[:], in_=s.dwv[g]), inc="qw", k=16)
                hj = j % 2
                is_q = j < 2
                tcs = range(2) if is_q else range(3)
                em.wait("tensor", "qw", mload)
                for tc in tcs:
                    bank = bank_i % 4
                    bank_i += 1
                    self.wait_bank(bank)
                    mm = None
                    for kt in range(NFC):
                        if is_q:
                            rhs = s.xn_own[:, kt * TH + tc * 512:kt * TH + (tc + 1) * 512]
                        elif tc == 0:
                            rhs = s.xn_halo[:, kt * HALO:(kt + 1) * HALO]
                        else:
                            rhs = s.xn_own[:, kt * TH + (tc - 1) * 512:kt * TH + tc * 512]
                        mm = em.op("tensor", lambda e, kt=kt, slot=slot, rhs=rhs, bank=bank: e.matmul(
                            s.ps[bank][:],
                            wqk_sb[:, slot * NFC * 128 + kt * 128:slot * NFC * 128 + (kt + 1) * 128],
                            rhs, start=(kt == 0), stop=(kt == NFC - 1)),
                            inc="pe" if kt == NFC - 1 else None)
                    wqk_slot_free[slot] = mm
                    em.wait("scalar", "pe", mm)
                    em.wait("scalar", "pe", qk_buf_free)
                    if is_q:
                        dst = q_buf[:, hj * TH + tc * 512:hj * TH + (tc + 1) * 512]
                    else:
                        dst = k_buf[:, hj * 1536 + tc * 512:hj * 1536 + (tc + 1) * 512]
                    ma = em.op("scalar", lambda e, dst=dst, bank=bank: e.copy(dst, s.ps[bank][:]), inc="act")
                    self.use_bank(bank, "act", ma)
                    proj_evacs.append(ma)

            em.wait("tensor", "qw", m_wv)
            v_evacs = []
            for tc in range(12):
                bank = bank_i % 4
                bank_i += 1
                self.wait_bank(bank)
                mm = None
                for kt in range(NFC):
                    if tc < 4:
                        lhsT = s.xn_halo[:, kt * HALO + tc * 128:kt * HALO + (tc + 1) * 128]
                    else:
                        lhsT = s.xn_own[:, kt * TH + (tc - 4) * 128:kt * TH + (tc - 3) * 128]
                    mm = em.op("tensor", lambda e, kt=kt, lhsT=lhsT, bank=bank: e.matmul(
                        s.ps[bank][:, 0:256], lhsT, wv_sb[:, kt * 256:(kt + 1) * 256],
                        start=(kt == 0), stop=(kt == NFC - 1)),
                        inc="pe" if kt == NFC - 1 else None)
                wv_free = mm
                em.wait("vector", "pe", mm)
                em.wait("vector", "pe", v_buf_free)
                mv = em.op("vector", lambda e, tc=tc, bank=bank: e.tensor_copy(
                    v_buf[:, tc * 256:(tc + 1) * 256], s.ps[bank][:, 0:256]), inc="vec")
                self.use_bank(bank, "vec", mv)
                v_evacs.append(mv)

            proj_done_act = max(proj_evacs)
            v_done_vec = max(v_evacs)
            if DEBUG and half == 0 and g == 0:
                em.wait("sync", "act", proj_done_act)
                em.wait("sync", "vec", v_done_vec)
                em.op("sync", lambda e, q_buf=q_buf: e.dma_start(out=s.dbg_q, in_=q_buf[:]), inc="st", k=16)
                em.op("sync", lambda e, k_buf=k_buf: e.dma_start(out=s.dbg_k, in_=k_buf[:]), inc="st", k=16)
                em.op("sync", lambda e, v_buf=v_buf: e.dma_start(out=s.dbg_v, in_=v_buf[:]), inc="st", k=16)

            last_av = 0
            for hj in range(2):
                for n in range(2):
                    unit_idx += 1
                    em.wait("tensor", "act", proj_done_act)
                    em.wait("tensor", "vec", v_done_vec)
                    exp_marks = []
                    mask_marks = []
                    den_m = av_m = None
                    for jt in range(8):
                        sbank = jt % 4
                        self.wait_bank(sbank)
                        sm = em.op("tensor", lambda e, hj=hj, n=n, jt=jt, sbank=sbank: e.matmul(
                            s.ps[sbank][:],
                            k_buf[:, hj * 1536 + n * 512 + jt * 128:hj * 1536 + n * 512 + (jt + 1) * 128],
                            q_buf[:, hj * TH + n * 512:hj * TH + (n + 1) * 512],
                            start=True, stop=True), inc="pe")
                        em.wait("scalar", "pe", sm)
                        ea = em.op("scalar", lambda e, jt=jt, sbank=sbank: e.activation(
                            p_buf[:, jt * 512:(jt + 1) * 512], s.ps[sbank][:], AF.Exp, scale=SCALE), inc="act")
                        self.use_bank(sbank, "act", ea)
                        exp_marks.append(ea)
                        em.wait("vector", "act", ea)
                        em.wait("vector", "qx", s.m_maskf)
                        if half == 0 and n == 0 and jt < 4:
                            mask_ap = s.mask_first[:, jt * 512:(jt + 1) * 512]
                        else:
                            mask_ap = s.mask_std[:, jt * 512:(jt + 1) * 512]
                        mk = em.op("vector", lambda e, jt=jt, mask_ap=mask_ap: e.tensor_mul(
                            p_buf[:, jt * 512:(jt + 1) * 512],
                            p_buf[:, jt * 512:(jt + 1) * 512], mask_ap), inc="vec")
                        mask_marks.append(mk)
                        em.wait("tensor", "vec", mk)
                        if jt == 0:
                            self.wait_bank(4)
                            self.wait_bank(5)
                            em.wait("tensor", "vec", s.m_ones128)
                        den_m = em.op("tensor", lambda e, jt=jt: e.matmul(
                            s.ps[5][:], s.ones128[:], p_buf[:, jt * 512:(jt + 1) * 512],
                            start=(jt == 0), stop=(jt == 7)),
                            inc="pe" if jt == 7 else None)
                        av_m = em.op("tensor", lambda e, hj=hj, n=n, jt=jt: e.matmul(
                            s.ps[4][:], v_buf[:, (n * 4 + jt) * 256 + hj * 128:(n * 4 + jt) * 256 + (hj + 1) * 128],
                            p_buf[:, jt * 512:(jt + 1) * 512],
                            start=(jt == 0), stop=(jt == 7)),
                            inc="pe" if jt == 7 else None)
                    last_av = av_m
                    if DEBUG and half == 0 and g == 0 and hj == 0 and n == 0:
                        em.wait("sync", "vec", max(mask_marks))
                        em.op("sync", lambda e, p_buf=p_buf: e.dma_start(out=s.dbg_p, in_=p_buf[:]), inc="st", k=16)
                    em.wait("vector", "pe", den_m)
                    rm = em.op("vector", lambda e: e.reciprocal(rden[:], s.ps[5][0:1, :]), inc="vec")
                    self.use_bank(5, "vec", rm)
                    em.wait("sync", "vec", rm)
                    em.wait("sync", "qw", getattr(s, "rden_read_mark", 0))
                    m_out = em.op("sync", lambda e: e.dma_start(out=s.rden_dram, in_=rden[:]), inc="qx", k=16)
                    em.wait("gpsimd", "qx", m_out)
                    em.wait("gpsimd", "vec", getattr(s, "bc_free", 0))
                    m_bc = em.op("gpsimd", lambda e: e.dma_start(
                        out=bc_sb[:], in_=s.rden_dram.to_broadcast([128, 512])), inc="qw", k=16)
                    s.rden_read_mark = m_bc
                    hslot = (2 * g + hj) % 4
                    em.wait("vector", "pe", av_m)
                    em.wait("vector", "qw", m_bc)
                    em.wait("vector", "pe", oproj_end_pe)
                    ov = em.op("vector", lambda e, hslot=hslot, n=n: e.tensor_tensor(
                        o_att[:, hslot * TH + n * 512:hslot * TH + (n + 1) * 512],
                        s.ps[4][:], bc_sb[:], ALU.mult), inc="vec")
                    self.use_bank(4, "vec", ov)
                    s.bc_free = ov
                    o_slot_written[(hslot, n)] = ov
            qk_buf_free = last_av
            v_buf_free = last_av

            if g % 2 == 1:
                passi = g // 2
                o_ready = max(o_slot_written.values())
                if DEBUG and half == 0 and g == 1:
                    em.wait("sync", "vec", o_ready)
                    em.op("sync", lambda e, o_att=o_att: e.dma_start(out=s.dbg_oatt, in_=o_att[:]), inc="st", k=16)
                wo_loads = []
                for dk in range(4):
                    em.wait("gpsimd", "pe", oproj_end_pe)
                    ml = em.op("gpsimd", lambda e, passi=passi, dk=dk: e.dma_start(
                        out=wo_sb[:, dk * D:(dk + 1) * D],
                        in_=s.dwout[passi * 4 + dk]), inc="qw", k=16)
                    wo_loads.append(ml)
                last_pe = 0
                em.wait("tensor", "vec", o_ready)
                em.wait("tensor", "qw", wo_loads[-1])
                for tt8 in range(8):
                    banks = [(tt8 * 4 + i) % 6 for i in range(4)] if False else [0, 1, 2, 3]
                    for o in range(4):
                        self.wait_bank(banks[o])
                    mmend = None
                    for dk in range(4):
                        for o in range(4):
                            mmend = em.op("tensor", lambda e, dk=dk, tt8=tt8, o=o, b=banks[o]: e.matmul(
                                s.ps[b][:],
                                o_att[:, dk * TH + tt8 * 128:dk * TH + (tt8 + 1) * 128],
                                wo_sb[:, dk * D + o * 512:dk * D + (o + 1) * 512],
                                start=(dk == 0), stop=(dk == 3)),
                                inc="pe" if (dk == 3 and o == 3) else None)
                    last_pe = mmend
                    em.wait("vector", "pe", mmend)
                    for o in range(4):
                        mv = em.op("vector", lambda e, tt8=tt8, o=o, b=banks[o]: e.tensor_add(
                            s.acc[:, tt8 * D + o * 512:tt8 * D + (o + 1) * 512],
                            s.acc[:, tt8 * D + o * 512:tt8 * D + (o + 1) * 512],
                            s.ps[b][:]), inc="vec")
                        self.use_bank(banks[o], "vec", mv)
                oproj_end_pe = last_pe
                if DEBUG and half == 0 and g == 7:
                    em.wait("sync", "vec", em.cnt["vec"])
                    em.op("sync", lambda e: e.dma_start(out=s.dbg_acc, in_=s.acc[:]), inc="st", k=16)
        s.attn_end_pe = em.cnt["pe"]

    def plan_moe(self, half, a_sc, up_sb, dn_sb, cwb_sb, sg_sb, us_sb):
        s, em = self, self.em
        self.phase_barrier()
        up_slot_free = [0, 0]
        dn_slot_free = [0, 0]
        a_tile_mark = {}
        a_sc_free = 0
        sgus_free = [0, 0]
        cwb_free = 0
        store_marks = []

        for ei in range(NEXP + 2):
            is_shared = ei >= NEXP
            part = ei - NEXP
            if not is_shared:
                em.wait("sync", "vec", s.cw_mark)
                em.wait("sync", "qw", getattr(s, "cw_read_mark", 0))
                m_cwo = em.op("sync", lambda e, ei=ei: e.dma_start(
                    out=s.cw_dram, in_=s.cw_T[ei:ei + 1, :]), inc="qx", k=16)
                em.wait("gpsimd", "qx", m_cwo)
                em.wait("gpsimd", "vec", cwb_free)
                cwb_ready = em.op("gpsimd", lambda e: e.dma_start(
                    out=cwb_sb[:], in_=s.cw_dram.to_broadcast([128, TH])), inc="qw", k=16)
                s.cw_read_mark = cwb_ready

            last_umul = 0
            for mc in range(NIC):
                slot = mc % 2
                em.wait("gpsimd", "pe", up_slot_free[slot])
                src = s.dsup[part, mc] if is_shared else s.dup[ei, mc]
                ml = em.op("gpsimd", lambda e, src=src, slot=slot: e.dma_start(
                    out=up_sb[:, slot * NFC * 256:(slot + 1) * NFC * 256], in_=src), inc="qw", k=16)
                em.wait("tensor", "qw", ml)
                for tt in range(2):
                    pslot = (mc * 2 + tt) % 2
                    gbank, ubank = 2 * pslot, 2 * pslot + 1
                    self.wait_bank(gbank)
                    self.wait_bank(ubank)
                    mg = mu = None
                    for kt in range(NFC):
                        base = slot * NFC * 256 + kt * 256
                        rhs = s.xn_own[:, kt * TH + tt * 512:kt * TH + (tt + 1) * 512]
                        mg = em.op("tensor", lambda e, base=base, rhs=rhs, gbank=gbank, kt=kt: e.matmul(
                            s.ps[gbank][:], up_sb[:, base:base + 128], rhs,
                            start=(kt == 0), stop=(kt == NFC - 1)),
                            inc="pe" if kt == NFC - 1 else None)
                        mu = em.op("tensor", lambda e, base=base, rhs=rhs, ubank=ubank, kt=kt: e.matmul(
                            s.ps[ubank][:], up_sb[:, base + 128:base + 256], rhs,
                            start=(kt == 0), stop=(kt == NFC - 1)),
                            inc="pe" if kt == NFC - 1 else None)
                    up_slot_free[slot] = mu
                    em.wait("scalar", "pe", mg)
                    em.wait("scalar", "vec", sgus_free[pslot])
                    msg = em.op("scalar", lambda e, gbank=gbank, pslot=pslot: e.activation(
                        sg_sb[:, pslot * 512:(pslot + 1) * 512], s.ps[gbank][:], AF.Silu), inc="act")
                    self.use_bank(gbank, "act", msg)
                    em.wait("vector", "pe", mu)
                    em.wait("vector", "vec", sgus_free[pslot])
                    if is_shared:
                        mus = em.op("vector", lambda e, ubank=ubank, pslot=pslot: e.tensor_copy(
                            us_sb[:, pslot * 512:(pslot + 1) * 512], s.ps[ubank][:]), inc="vec")
                    else:
                        em.wait("vector", "qw", cwb_ready)
                        mus = em.op("vector", lambda e, ubank=ubank, pslot=pslot, tt=tt: e.tensor_tensor(
                            us_sb[:, pslot * 512:(pslot + 1) * 512], s.ps[ubank][:],
                            cwb_sb[:, tt * 512:(tt + 1) * 512], ALU.mult), inc="vec")
                    self.use_bank(ubank, "vec", mus)
                    last_umul = mus
                    em.wait("vector", "act", msg)
                    em.wait("vector", "vec", mus)
                    em.wait("vector", "pe", a_sc_free)
                    mam = em.op("vector", lambda e, mc=mc, tt=tt, pslot=pslot: e.tensor_mul(
                        a_sc[:, mc * TH + tt * 512:mc * TH + (tt + 1) * 512],
                        sg_sb[:, pslot * 512:(pslot + 1) * 512],
                        us_sb[:, pslot * 512:(pslot + 1) * 512]), inc="vec")
                    sgus_free[pslot] = mam
                    a_tile_mark[(mc, tt)] = mam
            if not is_shared:
                cwb_free = last_umul

            a_ready = [max(a_tile_mark[(k, tt)] for k in range(NIC)) for tt in range(2)]
            last_down_ldw = 0
            for o in range(4):
                slot = o % 2
                em.wait("gpsimd", "pe", dn_slot_free[slot])
                srcd = s.dsdown[part, o] if is_shared else s.ddown[ei, o]
                mld = em.op("gpsimd", lambda e, srcd=srcd, slot=slot: e.dma_start(
                    out=dn_sb[:, slot * NIC * 512:(slot + 1) * NIC * 512], in_=srcd), inc="qw", k=16)
                em.wait("tensor", "qw", mld)
                for tt8 in range(8):
                    bank = 4 + (tt8 % 2)
                    self.wait_bank(bank)
                    em.wait("tensor", "vec", a_ready[tt8 // 4])
                    md = None
                    for k in range(NIC):
                        md = em.op("tensor", lambda e, k=k, tt8=tt8, slot=slot, o=o, bank=bank: e.matmul(
                            s.ps[bank][:],
                            a_sc[:, k * TH + tt8 * 128:k * TH + (tt8 + 1) * 128],
                            dn_sb[:, slot * NIC * 512 + k * 512:slot * NIC * 512 + (k + 1) * 512],
                            start=(k == 0), stop=(k == NIC - 1)),
                            inc="pe" if k == NIC - 1 else None)
                    dn_slot_free[slot] = md
                    last_down_ldw = md
                    em.wait("vector", "pe", md)
                    mv = em.op("vector", lambda e, tt8=tt8, o=o, bank=bank: e.tensor_add(
                        s.acc[:, tt8 * D + o * 512:tt8 * D + (o + 1) * 512],
                        s.acc[:, tt8 * D + o * 512:tt8 * D + (o + 1) * 512],
                        s.ps[bank][:]), inc="vec")
                    self.use_bank(bank, "vec", mv)
                    if is_shared and part == 1:
                        a_tile_mark[("add", o, tt8)] = mv
            a_sc_free = last_down_ldw

        for tt8 in range(8):
            last = max(a_tile_mark[("add", o, tt8)] for o in range(4))
            em.wait("sync", "vec", last)
            ms = em.op("sync", lambda e, tt8=tt8, half=half: e.dma_start(
                out=s.dy[half * TH + tt8 * 128:half * TH + (tt8 + 1) * 128, :],
                in_=s.acc[:, tt8 * D:(tt8 + 1) * D]), inc="st", k=16)
            store_marks.append(ms)
        s.half0_end_pe = em.cnt["pe"]



def _host_layouts(w_qkv, w_out, gate_w, up_w, down_w, sup_w, sdown_w):
    f32 = np.float32
    wqk = np.empty((NGRP, 4, 128, NFC * 128), f32)
    wq = w_qkv[0:D].reshape(NH, 128, NFC, 128)
    wk = w_qkv[D:2 * D].reshape(NH, 128, NFC, 128)
    for g in range(NGRP):
        for j in range(4):
            src = wq if j < 2 else wk
            h = 2 * g + (j % 2)
            wqk[g, j] = src[h].transpose(2, 1, 0).reshape(128, NFC * 128)
    wv = w_qkv[2 * D:3 * D].reshape(NH, 128, NFC, 128)
    wv_blk = np.empty((NGRP, 128, NFC * 256), f32)
    for g in range(NGRP):
        pair = np.stack([wv[2 * g], wv[2 * g + 1]], axis=0)
        wv_blk[g] = pair.transpose(3, 2, 0, 1).reshape(128, NFC * 256)
    wout_blk = np.ascontiguousarray(
        w_out.reshape(D, NFC, 128).transpose(1, 2, 0))
    gate_blk = np.ascontiguousarray(
        gate_w.reshape(16, NFC, 128).transpose(2, 1, 0).reshape(128, NFC * 16))
    up_blk = np.ascontiguousarray(
        up_w.reshape(NEXP, 2, NIC, 128, NFC, 128)
        .transpose(0, 2, 5, 4, 1, 3)
        .reshape(NEXP, NIC, 128, NFC * 256))
    down_blk = np.ascontiguousarray(
        down_w.reshape(NEXP, 4, 512, NIC, 128)
        .transpose(0, 1, 4, 3, 2)
        .reshape(NEXP, 4, 128, NIC * 512))
    sup_blk = np.ascontiguousarray(
        sup_w.reshape(2, 2, NIC, 128, NFC, 128)
        .transpose(1, 2, 5, 4, 0, 3)
        .reshape(2, NIC, 128, NFC * 256))
    sdown_blk = np.ascontiguousarray(
        sdown_w.reshape(4, 512, 2, NIC, 128)
        .transpose(2, 0, 4, 3, 1)
        .reshape(2, 4, 128, NIC * 512))
    return wqk, wv_blk, wout_blk, gate_blk, up_blk, down_blk, sup_blk, sdown_blk


def _masks():
    ka = np.arange(1024)[:, None]
    qa = np.arange(512)[None, :]
    band = ((qa <= ka) & (ka <= qa + 512)).astype(np.float32)
    m = band.reshape(8, 128, 512).astype(ml_dtypes.bfloat16)
    return m


_BUILT = {}


def _get_program():
    if "nc" not in _BUILT:
        b = Builder()
        _BUILT["nc"] = b.build()
    return _BUILT["nc"]


def kernel(L0_attn_x, L0_attn_w_qkv, L0_attn_w_out, L0_ffn_gate_w,
           L0_ffn_expert_up_w, L0_ffn_expert_down_w,
           L0_ffn_shared_up_w, L0_ffn_shared_down_w):
    x = np.asarray(L0_attn_x, np.float32)
    (wqk, wv_blk, wout_blk, gate_blk, up_blk,
     down_blk, sup_blk, sdown_blk) = _host_layouts(
        np.asarray(L0_attn_w_qkv, np.float32),
        np.asarray(L0_attn_w_out, np.float32),
        np.asarray(L0_ffn_gate_w, np.float32),
        np.asarray(L0_ffn_expert_up_w, np.float32),
        np.asarray(L0_ffn_expert_down_w, np.float32),
        np.asarray(L0_ffn_shared_up_w, np.float32),
        np.asarray(L0_ffn_shared_down_w, np.float32))
    mask_std = _masks()
    mask_first = np.ascontiguousarray(mask_std[0:4])
    mask_first0 = np.zeros_like(mask_first)
    ident32 = np.eye(128, dtype=np.float32)
    ident_bf = np.eye(128, dtype=ml_dtypes.bfloat16)

    shared = {
        "mask_std": mask_std, "ident_f32": ident32, "ident_bf": ident_bf,
        "wqk_blk": wqk, "wv_blk": wv_blk, "wout_blk": wout_blk,
        "gate_blk": gate_blk, "up_blk": up_blk, "down_blk": down_blk,
        "sup_blk": sup_blk, "sdown_blk": sdown_blk,
    }
    in_maps = []
    for c in range(8):
        b, qt = c // 4, c % 4
        rows = x[b, qt * T_OWN:(qt + 1) * T_OWN, :]
        if qt == 0:
            halo = np.zeros((HALO, D), np.float32)
            mf = mask_first0
        else:
            halo = x[b, qt * T_OWN - HALO:qt * T_OWN, :]
            mf = mask_first
        im = dict(shared)
        im["x_own"] = np.ascontiguousarray(rows)
        im["x_halo"] = np.ascontiguousarray(halo)
        im["mask_first"] = mf
        in_maps.append(im)

    nc = _get_program()
    res = run_bass_kernel_spmd(nc, in_maps, core_ids=list(range(8)))
    out = np.empty((2, 8192, D), np.float32)
    for c in range(8):
        b, qt = c // 4, c % 4
        out[b, qt * T_OWN:(qt + 1) * T_OWN, :] = res.results[c]["y"]
    if DEBUG:
        global last_debug
        last_debug = res.results
    return out


last_debug = None


# version 08
# speedup vs baseline: 1.1806x; 1.1443x over previous
import math
import numpy as np
import ml_dtypes
from contextlib import ExitStack

import concourse.bass as bass
import concourse.mybir as mybir
from concourse.bass_utils import run_bass_kernel_spmd

F32 = mybir.dt.float32
BF16 = mybir.dt.bfloat16
AF = mybir.ActivationFunctionType
ALU = mybir.AluOpType
AX = mybir.AxisListType

D = 2048
NH = 16
HD = 128
NFC = 16
T_OWN = 2048
TH = 1024
HALO = 512
SCALE = 1.0 / math.sqrt(HD)
NEXP = 16
NIC = 11
NGRP = 8

DEBUG = False

ENGINES = ("sync", "gpsimd", "vector", "scalar", "tensor")
SEMS = ("pe", "act", "vec", "pool", "qw", "qx", "st")


class Em:

    def __init__(self):
        self.plan = {e: [] for e in ENGINES}
        self.meta = {e: [] for e in ENGINES}
        self.cnt = {s: 0 for s in SEMS}
        self.wmax = {}
        self.sems = {}

    def emit(self, eng, fn, meta=("o",)):
        self.plan[eng].append(fn)
        self.meta[eng].append(meta)

    def wait(self, eng, sem, val):
        if val is None or val <= 0:
            return
        assert val <= self.cnt[sem], f"wait on future {sem}: {val} > {self.cnt[sem]}"
        key = (eng, sem)
        if self.wmax.get(key, 0) >= val:
            return
        self.wmax[key] = val
        self.emit(eng, lambda e, s=sem, v=val: e.wait_ge(self.sems[s], v), ("w", sem, val))

    def op(self, eng, build, inc=None, k=1):
        if inc is None:
            self.emit(eng, lambda e, b=build: b(e))
            return None
        self.cnt[inc] += k
        m = self.cnt[inc]
        self.emit(eng, lambda e, b=build, s=inc, kk=k: b(e).then_inc(self.sems[s], kk),
                  ("i", inc, k))
        return m

    def simulate(self):
        pos = {e: 0 for e in ENGINES}
        sem = {x: 0 for x in SEMS}
        progress = True
        while progress:
            progress = False
            for e in ENGINES:
                while pos[e] < len(self.meta[e]):
                    m = self.meta[e][pos[e]]
                    if m[0] == "w" and sem[m[1]] < m[2]:
                        break
                    if m[0] == "i":
                        sem[m[1]] += m[2]
                    pos[e] += 1
                    progress = True
        stuck = {e: (pos[e], len(self.meta[e])) for e in ENGINES if pos[e] < len(self.meta[e])}
        if stuck:
            detail = []
            for e, (p, n) in stuck.items():
                detail.append(f"{e}@{p}/{n} blocked on {self.meta[e][p]}")
            raise RuntimeError("schedule deadlock: " + "; ".join(detail))


class Builder:
    def __init__(self):
        self.nc = bass.Bass("TRN2", target_bir_lowering=False, debug=False)
        self.em = Em()

    def build(self):
        nc = self.nc
        em = self.em
        s = self

        s.dx = nc.dram_tensor("x_own", [T_OWN, D], F32, kind="ExternalInput").ap()
        s.dxh = nc.dram_tensor("x_halo", [HALO, D], F32, kind="ExternalInput").ap()
        s.dmask = nc.dram_tensor("mask_std", [8, 128, 512], BF16, kind="ExternalInput").ap()
        s.dmaskf = nc.dram_tensor("mask_first", [4, 128, 512], BF16, kind="ExternalInput").ap()
        s.dident32 = nc.dram_tensor("ident_f32", [128, 128], F32, kind="ExternalInput").ap()
        s.dident_bf = nc.dram_tensor("ident_bf", [128, 128], BF16, kind="ExternalInput").ap()
        s.dwqk = nc.dram_tensor("wqk_blk", [NGRP, 4, 128, NFC * 128], F32, kind="ExternalInput").ap()
        s.dwv = nc.dram_tensor("wv_blk", [NGRP, 128, NFC * 256], F32, kind="ExternalInput").ap()
        s.dwout = nc.dram_tensor("wout_blk", [NFC, 128, D], F32, kind="ExternalInput").ap()
        s.dgate = nc.dram_tensor("gate_blk", [128, NFC * 16], F32, kind="ExternalInput").ap()
        s.dup = nc.dram_tensor("up_blk", [NEXP, NIC, 128, NFC * 256], F32, kind="ExternalInput").ap()
        s.ddown = nc.dram_tensor("down_blk", [NEXP, 4, 128, NIC * 512], F32, kind="ExternalInput").ap()
        s.dsup = nc.dram_tensor("sup_blk", [2, NIC, 128, NFC * 256], F32, kind="ExternalInput").ap()
        s.dsdown = nc.dram_tensor("sdown_blk", [2, 4, 128, NIC * 512], F32, kind="ExternalInput").ap()
        s.dy = nc.dram_tensor("y", [T_OWN, D], F32, kind="ExternalOutput").ap()
        s.rden_dram = nc.dram_tensor("rden_scr", [1, 512], F32, kind="Internal").ap()
        s.cw_dram = nc.dram_tensor("cw_scr", [1, TH], F32, kind="Internal").ap()
        if DEBUG:
            s.dbg_xn = nc.dram_tensor("dbg_xn", [128, NFC * TH], BF16, kind="ExternalOutput").ap()
            s.dbg_xnh = nc.dram_tensor("dbg_xnh", [128, NFC * HALO], BF16, kind="ExternalOutput").ap()
            s.dbg_q = nc.dram_tensor("dbg_q", [128, 2 * TH], BF16, kind="ExternalOutput").ap()
            s.dbg_k = nc.dram_tensor("dbg_k", [128, 2 * (TH + HALO)], BF16, kind="ExternalOutput").ap()
            s.dbg_v = nc.dram_tensor("dbg_v", [128, 12 * 256], BF16, kind="ExternalOutput").ap()
            s.dbg_p = nc.dram_tensor("dbg_p", [128, 8 * 512], BF16, kind="ExternalOutput").ap()
            s.dbg_oatt = nc.dram_tensor("dbg_oatt", [128, 8 * TH], BF16, kind="ExternalOutput").ap()
            s.dbg_acc = nc.dram_tensor("dbg_acc", [128, 8 * D], F32, kind="ExternalOutput").ap()
            s.dbg_cw = nc.dram_tensor("dbg_cw", [16, TH], F32, kind="ExternalOutput").ap()

        with ExitStack() as ctx:
            E = ctx.enter_context
            s.acc = E(nc.sbuf_tensor("acc", [128, 8 * D], F32))
            s.xn_own = E(nc.sbuf_tensor("xn_own", [128, NFC * TH], BF16))
            s.xn_halo = E(nc.sbuf_tensor("xn_halo", [128, NFC * HALO], BF16))
            s.mask_std = E(nc.sbuf_tensor("mask_std_sb", [128, 8 * 512], BF16))
            s.mask_first = E(nc.sbuf_tensor("mask_first_sb", [128, 4 * 512], BF16))
            s.ident32 = E(nc.sbuf_tensor("ident_f32_sb", [128, 128], F32))
            s.ident_bf = E(nc.sbuf_tensor("ident_bf_sb", [128, 128], BF16))
            s.ones_col = E(nc.sbuf_tensor("ones_col", [128, 1], BF16))
            s.ones128 = E(nc.sbuf_tensor("ones128", [128, 128], BF16))
            s.ones_row = E(nc.sbuf_tensor("ones_row", [1, 128], F32))
            s.gate_sb = E(nc.sbuf_tensor("gate_sb", [128, NFC * 16], BF16))
            s.cw_T = E(nc.sbuf_tensor("cw_T", [16, TH], F32))
            s.ps = [E(nc.psum_tensor(f"ps{i}", [128, 512], F32)) for i in range(6)]
            s.ptr = [E(nc.psum_tensor(f"ptr{i}", [128, 1024], BF16)) for i in range(2)]
            for nm in SEMS:
                em.sems[nm] = E(nc.semaphore(nm))

            s.bank_free = {i: ("vec", 0) for i in range(6)}
            s.ptr_free = {i: ("vec", 0) for i in range(2)}

            self.plan_init()
            for half in (0, 1):
                self.plan_half(half)
            em.wait("sync", "st", em.cnt["st"])

            self.em.simulate()
            self.replay()
        return nc

    def replay(self):
        nc, em = self.nc, self.em
        with nc.Block() as block:
            for eng_name in ENGINES:
                cmds = em.plan[eng_name]

                def runner(e, cmds=cmds):
                    for c in cmds:
                        c(e)
                getattr(block, eng_name)(runner)

    def phase_barrier(self):
        em = self.em
        pe, act, vec = em.cnt["pe"], em.cnt["act"], em.cnt["vec"]
        for q in ("gpsimd", "sync"):
            em.wait(q, "pe", pe)
            em.wait(q, "act", act)
            em.wait(q, "vec", vec)
        em.wait("vector", "pe", pe)
        em.wait("vector", "act", act)
        em.wait("scalar", "pe", pe)
        em.wait("scalar", "vec", vec)
        if DEBUG:
            st = em.cnt["st"]
            for q in ("gpsimd", "sync", "vector", "scalar", "tensor"):
                em.wait(q, "st", st)

    def wait_bank(self, bank):
        sem, val = self.bank_free[bank]
        self.em.wait("tensor", sem, val)

    def use_bank(self, bank, sem, val):
        self.bank_free[bank] = (sem, val)

    def plan_init(self):
        s, em = self, self.em
        s.m_masks = em.op("sync", lambda e: e.dma_start(
            out=s.mask_std[:], in_=s.dmask.rearrange("j p c -> p j c")), inc="qx", k=16)
        s.m_maskf = em.op("sync", lambda e: e.dma_start(
            out=s.mask_first[:], in_=s.dmaskf.rearrange("j p c -> p j c")), inc="qx", k=16)
        s.m_id32 = em.op("sync", lambda e: e.dma_start(out=s.ident32[:], in_=s.dident32), inc="qx", k=16)
        s.m_id_bf = em.op("sync", lambda e: e.dma_start(out=s.ident_bf[:], in_=s.dident_bf), inc="qx", k=16)
        s.m_gate = em.op("gpsimd", lambda e: e.dma_start(out=s.gate_sb[:], in_=s.dgate), inc="qw", k=16)
        s.m_onescol = em.op("vector", lambda e: e.memset(s.ones_col[:], 1.0), inc="vec")
        s.m_ones128 = em.op("vector", lambda e: e.memset(s.ones128[:], 1.0), inc="vec")
        s.m_ones = em.op("vector", lambda e: e.memset(s.ones_row[:], 1.0), inc="vec")

    def plan_half(self, half):
        s, em, nc = self, self.em, self.nc
        s.half = half

        with ExitStack() as c1:
            E = c1.enter_context
            H = f"_h{half}"
            xh_t = E(nc.sbuf_tensor("xh_t" + H, [128, D], F32))
            lnt = E(nc.sbuf_tensor("lnt" + H, [128, 2 * D], BF16))
            st6 = E(nc.sbuf_tensor("st6" + H, [128, 24], F32))
            st2 = E(nc.sbuf_tensor("st2" + H, [128, 2], F32))
            nmean = E(nc.sbuf_tensor("nmean" + H, [128, 1], F32))
            rstd = E(nc.sbuf_tensor("rstd" + H, [128, 1], F32))
            glg = E(nc.sbuf_tensor("glg" + H, [128, 16], F32))
            gmx = E(nc.sbuf_tensor("gmx" + H, [128, 1], F32))
            gex = E(nc.sbuf_tensor("gex" + H, [128, 16], F32))
            gsum = E(nc.sbuf_tensor("gsum" + H, [128, 1], F32))
            gsc = E(nc.sbuf_tensor("gsc" + H, [128, 16], F32))
            ggs = E(nc.sbuf_tensor("ggs" + H, [128, 8], F32))
            gt8a = E(nc.sbuf_tensor("gt8a" + H, [128, 8], F32))
            gem_ = E(nc.sbuf_tensor("gem" + H, [128, 16], F32))
            gmk = E(nc.sbuf_tensor("gmk" + H, [128, 16], F32))
            gt8b = E(nc.sbuf_tensor("gt8b" + H, [128, 8], F32))
            gw = E(nc.sbuf_tensor("gw" + H, [128, 16], F32))
            gws = E(nc.sbuf_tensor("gws" + H, [128, 1], F32))

            s.ggs_init = em.op("vector", lambda e: e.memset(ggs[:, 4:8], -1e30), inc="vec")

            self.plan_ln(half, xh_t, lnt, st6, st2, nmean, rstd)
            self.plan_gate(half, glg, gmx, gex, gsum, gsc, ggs, gt8a, gem_, gmk, gt8b, gw, gws)

        with ExitStack() as c2:
            A = c2.enter_context
            H = f"_h{half}"
            q_buf = A(nc.sbuf_tensor("q_buf" + H, [128, 2 * TH], BF16))
            k_buf = A(nc.sbuf_tensor("k_buf" + H, [128, 2 * (TH + HALO)], BF16))
            v_buf = A(nc.sbuf_tensor("v_buf" + H, [128, 12 * 256], BF16))
            wqk_sb = A(nc.sbuf_tensor("wqk_sb" + H, [128, 2 * NFC * 128], BF16))
            wv_sb = A(nc.sbuf_tensor("wv_sb" + H, [128, NFC * 256], BF16))
            p_buf = A(nc.sbuf_tensor("p_buf" + H, [128, 8 * 512], BF16))
            o_att = A(nc.sbuf_tensor("o_att" + H, [128, 4 * TH], BF16))
            wo_sb = A(nc.sbuf_tensor("wo_sb" + H, [128, 4 * D], BF16))
            rden = A(nc.sbuf_tensor("rden" + H, [1, 512], F32))
            rden2 = A(nc.sbuf_tensor("rden2" + H, [1, 512], F32))
            bc_sb = A(nc.sbuf_tensor("bc_sb" + H, [128, 512], F32))
            self.plan_attn(half, q_buf, k_buf, v_buf, wqk_sb, wv_sb, p_buf, o_att, wo_sb, rden, rden2, bc_sb)

        with ExitStack() as c3:
            M = c3.enter_context
            H = f"_h{half}"
            a_sc = M(nc.sbuf_tensor("a_sc" + H, [128, NIC * TH], BF16))
            up_sb = M(nc.sbuf_tensor("up_sb" + H, [128, 2 * NFC * 256], BF16))
            dn_sb = M(nc.sbuf_tensor("dn_sb" + H, [128, 2 * NIC * 512], BF16))
            cwb_sb = M(nc.sbuf_tensor("cwb_sb" + H, [128, 2 * 512], F32))
            sg_sb = M(nc.sbuf_tensor("sg_sb" + H, [128, 2 * 512], BF16))
            us_sb = M(nc.sbuf_tensor("us_sb" + H, [128, 2 * 512], BF16))
            self.plan_moe(half, a_sc, up_sb, dn_sb, cwb_sb, sg_sb, us_sb)

    def plan_ln(self, half, xh_t, lnt, st6, st2, nmean, rstd):
        s, em = self, self.em
        self.phase_barrier()
        s.xn_mark = getattr(s, "xn_mark", {})
        ptr_i = 0
        lnt_free = [0, 0]
        lnt_free_sem = ["vec", "vec"]
        xh_free = ("vec", 0)

        if half == 1:
            em.wait("vector", "pe", s.half0_end_pe)
            last = None
            for fc in range(NFC):
                last = em.op("vector", lambda e, fc=fc: e.tensor_copy(
                    s.xn_halo[:, fc * HALO:(fc + 1) * HALO],
                    s.xn_own[:, fc * TH + 512:fc * TH + TH]),
                    inc="vec" if fc == NFC - 1 else None)
            s.halo_copy_mark = last

        tiles = ([("halo", i) for i in range(4)] if half == 0 else []) + \
                [("own", i) for i in range(8)]
        for kind, ti in tiles:
            if kind == "halo":
                em.wait("sync", xh_free[0], xh_free[1])
                mx = em.op("sync", lambda e, ti=ti: e.dma_start(
                    out=xh_t[:], in_=s.dxh[ti * 128:(ti + 1) * 128, :]), inc="qx", k=16)
                xsrc = xh_t[:]
            else:
                mx = em.op("sync", lambda e, ti=ti: e.dma_start(
                    out=s.acc[:, ti * D:(ti + 1) * D],
                    in_=s.dx[half * TH + ti * 128:half * TH + (ti + 1) * 128, :]), inc="qx", k=16)
                xsrc = s.acc[:, ti * D:(ti + 1) * D]

            em.wait("vector", "qx", mx)
            last = None
            for i in range(4):
                last = em.op("vector", lambda e, i=i, xsrc=xsrc: e.bn_stats(
                    st6[:, 6 * i:6 * i + 6], xsrc[:, 512 * i:512 * (i + 1)]),
                    inc="vec" if i == 3 else None)
            em.wait("vector", "vec", last)
            m1 = em.op("vector", lambda e: e.bn_aggr(st2[:], st6[:]), inc="vec")
            em.wait("vector", "vec", m1)
            em.op("vector", lambda e: e.tensor_scalar(nmean[:], st2[:, 0:1], -1.0, None, ALU.mult))
            m2 = em.op("vector", lambda e: e.tensor_scalar(rstd[:], st2[:, 1:2], 1e-5, None, ALU.add), inc="vec")
            em.wait("scalar", "vec", m2)
            a1 = em.op("scalar", lambda e: e.activation(rstd[:], rstd[:], AF.Ln), inc="act")
            em.wait("scalar", "act", a1)
            a2 = em.op("scalar", lambda e: e.activation(rstd[:], rstd[:], AF.Exp, scale=-0.5), inc="act")
            em.wait("vector", "act", a2)
            lslot = (ti + (0 if kind == "own" else 8)) % 2
            em.wait("vector", lnt_free_sem[lslot], lnt_free[lslot])
            m3 = em.op("vector", lambda e, xsrc=xsrc, lslot=lslot: e.tensor_scalar(
                lnt[:, lslot * D:(lslot + 1) * D], xsrc, nmean[:], rstd[:], ALU.add, ALU.mult), inc="vec")

            em.wait("tensor", "vec", m3)
            em.wait("tensor", "qx", s.m_id_bf)
            for b in range(4):
                bank = ptr_i % 2
                ptr_i += 1
                sem, val = s.ptr_free[bank]
                em.wait("tensor", sem, val)
                mt = None
                for j in range(4):
                    fc = 4 * b + j
                    mt = em.op("tensor", lambda e, fc=fc, j=j, bank=bank, lslot=lslot: e.transpose(
                        s.ptr[bank][:, j * 128:(j + 1) * 128],
                        lnt[:, lslot * D + fc * 128:lslot * D + (fc + 1) * 128],
                        s.ident_bf[:]),
                        inc="pe" if j == 3 else None)
                em.wait("vector", "pe", mt)
                if kind == "own":
                    dst = s.xn_own[:].rearrange("p (f t) -> p f t", t=TH)[:, 4 * b:4 * b + 4, ti * 128:(ti + 1) * 128]
                else:
                    dst = s.xn_halo[:].rearrange("p (f t) -> p f t", t=HALO)[:, 4 * b:4 * b + 4, ti * 128:(ti + 1) * 128]
                mv = em.op("vector", lambda e, dst=dst, bank=bank: e.tensor_copy(
                    dst, s.ptr[bank][:, 0:512].rearrange("p (f t) -> p f t", t=128)), inc="vec")
                s.ptr_free[bank] = ("vec", mv)
            s.xn_mark[(kind, ti)] = mv
            lnt_free[lslot], lnt_free_sem[lslot] = mt, "pe"
            if kind == "halo":
                xh_free = ("vec", m3)

        s.xn_all_own = max(s.xn_mark[("own", i)] for i in range(8))
        if half == 0:
            s.xn_all_halo = max(s.xn_mark[("halo", i)] for i in range(4))
        else:
            s.xn_all_halo = max(s.halo_copy_mark, 0)
        if DEBUG and half == 0:
            em.wait("sync", "vec", max(s.xn_all_own, s.xn_all_halo))
            em.op("sync", lambda e: e.dma_start(out=s.dbg_xn, in_=s.xn_own[:]), inc="st", k=16)
            em.op("sync", lambda e: e.dma_start(out=s.dbg_xnh, in_=s.xn_halo[:]), inc="st", k=16)

    def plan_gate(self, half, glg, gmx, gex, gsum, gsc, ggs, gt8a, gem_, gmk, gt8b, gw, gws):
        s, em = self, self.em
        B5 = 5

        def edge(m):
            em.wait("vector", "vec", m)

        for tt in range(8):
            em.wait("tensor", "vec", s.xn_mark[("own", tt)])
            em.wait("tensor", "qw", s.m_gate)
            self.wait_bank(B5)
            mg = None
            for kt in range(NFC):
                mg = em.op("tensor", lambda e, kt=kt, tt=tt: e.matmul(
                    s.ps[B5][:, 0:16],
                    s.xn_own[:, kt * TH + tt * 128:kt * TH + (tt + 1) * 128],
                    s.gate_sb[:, kt * 16:(kt + 1) * 16],
                    start=(kt == 0), stop=(kt == NFC - 1)),
                    inc="pe" if kt == NFC - 1 else None)
            em.wait("scalar", "pe", mg)
            ma = em.op("scalar", lambda e: e.copy(glg[:], s.ps[B5][:, 0:16]), inc="act")
            self.use_bank(B5, "act", ma)

            em.wait("vector", "act", ma)
            v1 = em.op("vector", lambda e: e.reduce_max(gmx[:], glg[:], AX.X, negate=True), inc="vec")
            em.wait("scalar", "vec", v1)
            a2 = em.op("scalar", lambda e: e.activation(gex[:], glg[:], AF.Exp, bias=gmx[:], scale=1.0), inc="act")
            em.wait("vector", "act", a2)
            edge(em.op("vector", lambda e: e.reduce_sum(gsum[:], gex[:], AX.X), inc="vec"))
            edge(em.op("vector", lambda e: e.reciprocal(gsum[:], gsum[:]), inc="vec"))
            edge(em.op("vector", lambda e: e.tensor_scalar(gsc[:], gex[:], gsum[:], None, ALU.mult), inc="vec"))
            em.wait("vector", "vec", s.ggs_init)
            edge(em.op("vector", lambda e: e.tensor_reduce(
                ggs[:, 0:4, None], gsc[:].rearrange("p (g e) -> p g e", e=4), AX.X, ALU.max), inc="vec"))
            edge(em.op("vector", lambda e: e.max(out=gt8a[:], in_=ggs[:]), inc="vec"))
            last = None
            for j in range(4):
                last = em.op("vector", lambda e, j=j: e.tensor_scalar(
                    gem_[:, j::4], ggs[:, 0:4], gt8a[:, 1:2], None, ALU.is_ge),
                    inc="vec" if j == 3 else None)
            edge(last)
            edge(em.op("vector", lambda e: e.tensor_mul(gmk[:], gsc[:], gem_[:]), inc="vec"))
            edge(em.op("vector", lambda e: e.max(out=gt8b[:], in_=gmk[:]), inc="vec"))
            edge(em.op("vector", lambda e: e.tensor_scalar(gw[:], gmk[:], gt8b[:, 3:4], None, ALU.is_ge), inc="vec"))
            edge(em.op("vector", lambda e: e.tensor_mul(gw[:], gw[:], gmk[:]), inc="vec"))
            edge(em.op("vector", lambda e: e.reduce_sum(gws[:], gw[:], AX.X), inc="vec"))
            edge(em.op("vector", lambda e: e.tensor_scalar(gws[:], gws[:], 1e-20, None, ALU.add), inc="vec"))
            edge(em.op("vector", lambda e: e.reciprocal(gws[:], gws[:]), inc="vec"))
            v2 = em.op("vector", lambda e: e.tensor_scalar(gw[:], gw[:], gws[:], None, ALU.mult), inc="vec")

            em.wait("tensor", "vec", v2)
            em.wait("tensor", "qx", s.m_id32)
            self.wait_bank(B5)
            mt = em.op("tensor", lambda e: e.transpose(
                s.ps[B5][0:16, 0:128], gw[:], s.ident32[:]), inc="pe")
            em.wait("vector", "pe", mt)
            mv = em.op("vector", lambda e, tt=tt: e.tensor_copy(
                s.cw_T[:, tt * 128:(tt + 1) * 128], s.ps[B5][0:16, 0:128]), inc="vec")
            self.use_bank(B5, "vec", mv)
        s.cw_mark = mv
        if DEBUG and half == 0:
            em.wait("sync", "vec", mv)
            em.op("sync", lambda e: e.dma_start(out=s.dbg_cw, in_=s.cw_T[:]), inc="st", k=16)

    def plan_attn(self, half, q_buf, k_buf, v_buf, wqk_sb, wv_sb, p_buf, o_att, wo_sb, rden, rden2, bc_sb):
        s, em = self, self.em
        self.phase_barrier()

        wqk_slot_free = [0, 0]
        wv_free = 0
        qk_buf_free = 0
        v_buf_free = 0
        o_slot_written = {}
        oproj_end_pe = 0
        unit_idx = 0

        em.wait("tensor", "vec", s.xn_all_own)
        em.wait("tensor", "vec", s.xn_all_halo)

        for g in range(NGRP):
            bank_i = 0
            proj_evacs = []
            for j in range(4):
                slot = j % 2
                em.wait("gpsimd", "pe", wqk_slot_free[slot])
                mload = em.op("gpsimd", lambda e, g=g, j=j, slot=slot: e.dma_start(
                    out=wqk_sb[:, slot * NFC * 128:(slot + 1) * NFC * 128],
                    in_=s.dwqk[g, j]), inc="qw", k=16)
                if j == 0:
                    em.wait("gpsimd", "pe", wv_free)
                    m_wv = em.op("gpsimd", lambda e, g=g: e.dma_start(out=wv_sb[:], in_=s.dwv[g]), inc="qw", k=16)
                hj = j % 2
                is_q = j < 2
                tcs = range(2) if is_q else range(3)
                em.wait("tensor", "qw", mload)
                for tc in tcs:
                    bank = bank_i % 4
                    bank_i += 1
                    self.wait_bank(bank)
                    mm = None
                    for kt in range(NFC):
                        if is_q:
                            rhs = s.xn_own[:, kt * TH + tc * 512:kt * TH + (tc + 1) * 512]
                        elif tc == 0:
                            rhs = s.xn_halo[:, kt * HALO:(kt + 1) * HALO]
                        else:
                            rhs = s.xn_own[:, kt * TH + (tc - 1) * 512:kt * TH + tc * 512]
                        mm = em.op("tensor", lambda e, kt=kt, slot=slot, rhs=rhs, bank=bank: e.matmul(
                            s.ps[bank][:],
                            wqk_sb[:, slot * NFC * 128 + kt * 128:slot * NFC * 128 + (kt + 1) * 128],
                            rhs, start=(kt == 0), stop=(kt == NFC - 1)),
                            inc="pe" if kt == NFC - 1 else None)
                    wqk_slot_free[slot] = mm
                    em.wait("scalar", "pe", mm)
                    em.wait("scalar", "pe", qk_buf_free)
                    if is_q:
                        dst = q_buf[:, hj * TH + tc * 512:hj * TH + (tc + 1) * 512]
                    else:
                        dst = k_buf[:, hj * 1536 + tc * 512:hj * 1536 + (tc + 1) * 512]
                    ma = em.op("scalar", lambda e, dst=dst, bank=bank: e.copy(dst, s.ps[bank][:]), inc="act")
                    self.use_bank(bank, "act", ma)
                    proj_evacs.append(ma)

            em.wait("tensor", "qw", m_wv)
            v_evacs = []
            for tc in range(12):
                bank = bank_i % 4
                bank_i += 1
                self.wait_bank(bank)
                mm = None
                for kt in range(NFC):
                    if tc < 4:
                        lhsT = s.xn_halo[:, kt * HALO + tc * 128:kt * HALO + (tc + 1) * 128]
                    else:
                        lhsT = s.xn_own[:, kt * TH + (tc - 4) * 128:kt * TH + (tc - 3) * 128]
                    mm = em.op("tensor", lambda e, kt=kt, lhsT=lhsT, bank=bank: e.matmul(
                        s.ps[bank][:, 0:256], lhsT, wv_sb[:, kt * 256:(kt + 1) * 256],
                        start=(kt == 0), stop=(kt == NFC - 1)),
                        inc="pe" if kt == NFC - 1 else None)
                wv_free = mm
                em.wait("vector", "pe", mm)
                em.wait("vector", "pe", v_buf_free)
                mv = em.op("vector", lambda e, tc=tc, bank=bank: e.tensor_copy(
                    v_buf[:, tc * 256:(tc + 1) * 256], s.ps[bank][:, 0:256]), inc="vec")
                self.use_bank(bank, "vec", mv)
                v_evacs.append(mv)

            proj_done_act = max(proj_evacs)
            v_done_vec = max(v_evacs)
            if DEBUG and half == 0 and g == 0:
                em.wait("sync", "act", proj_done_act)
                em.wait("sync", "vec", v_done_vec)
                em.op("sync", lambda e, q_buf=q_buf: e.dma_start(out=s.dbg_q, in_=q_buf[:]), inc="st", k=16)
                em.op("sync", lambda e, k_buf=k_buf: e.dma_start(out=s.dbg_k, in_=k_buf[:]), inc="st", k=16)
                em.op("sync", lambda e, v_buf=v_buf: e.dma_start(out=s.dbg_v, in_=v_buf[:]), inc="st", k=16)

            last_av = 0
            for hj in range(2):
                for n in range(2):
                    unit_idx += 1
                    em.wait("tensor", "act", proj_done_act)
                    em.wait("tensor", "vec", v_done_vec)
                    exp_marks = []
                    mask_marks = []
                    den_m = av_m = None
                    avbank = 3 + (unit_idx % 2)
                    for jt in range(8):
                        sbank = jt % 3
                        self.wait_bank(sbank)
                        sm = em.op("tensor", lambda e, hj=hj, n=n, jt=jt, sbank=sbank: e.matmul(
                            s.ps[sbank][:],
                            k_buf[:, hj * 1536 + n * 512 + jt * 128:hj * 1536 + n * 512 + (jt + 1) * 128],
                            q_buf[:, hj * TH + n * 512:hj * TH + (n + 1) * 512],
                            start=True, stop=True), inc="pe")
                        em.wait("scalar", "pe", sm)
                        ea = em.op("scalar", lambda e, jt=jt, sbank=sbank: e.activation(
                            p_buf[:, jt * 512:(jt + 1) * 512], s.ps[sbank][:], AF.Exp, scale=SCALE), inc="act")
                        self.use_bank(sbank, "act", ea)
                        exp_marks.append(ea)
                        em.wait("vector", "act", ea)
                        em.wait("vector", "qx", s.m_maskf)
                        if half == 0 and n == 0 and jt < 4:
                            mask_ap = s.mask_first[:, jt * 512:(jt + 1) * 512]
                        else:
                            mask_ap = s.mask_std[:, jt * 512:(jt + 1) * 512]
                        mk = em.op("vector", lambda e, jt=jt, mask_ap=mask_ap: e.tensor_mul(
                            p_buf[:, jt * 512:(jt + 1) * 512],
                            p_buf[:, jt * 512:(jt + 1) * 512], mask_ap), inc="vec")
                        mask_marks.append(mk)
                        em.wait("tensor", "vec", mk)
                        if jt == 0:
                            self.wait_bank(avbank)
                            self.wait_bank(5)
                            em.wait("tensor", "vec", s.m_ones128)
                        den_m = em.op("tensor", lambda e, jt=jt: e.matmul(
                            s.ps[5][:], s.ones128[:], p_buf[:, jt * 512:(jt + 1) * 512],
                            start=(jt == 0), stop=(jt == 7)),
                            inc="pe" if jt == 7 else None)
                        av_m = em.op("tensor", lambda e, hj=hj, n=n, jt=jt, avbank=avbank: e.matmul(
                            s.ps[avbank][:], v_buf[:, (n * 4 + jt) * 256 + hj * 128:(n * 4 + jt) * 256 + (hj + 1) * 128],
                            p_buf[:, jt * 512:(jt + 1) * 512],
                            start=(jt == 0), stop=(jt == 7)),
                            inc="pe" if jt == 7 else None)
                    last_av = av_m
                    if DEBUG and half == 0 and g == 0 and hj == 0 and n == 0:
                        em.wait("sync", "vec", max(mask_marks))
                        em.op("sync", lambda e, p_buf=p_buf: e.dma_start(out=s.dbg_p, in_=p_buf[:]), inc="st", k=16)
                    em.wait("scalar", "pe", den_m)
                    em.wait("scalar", "vec", getattr(s, "bc_free", 0))
                    al = em.op("scalar", lambda e: e.activation(bc_sb[:], s.ps[5][:], AF.Ln), inc="act")
                    self.use_bank(5, "act", al)
                    em.wait("scalar", "act", al)
                    ax = em.op("scalar", lambda e: e.activation(bc_sb[:], bc_sb[:], AF.Exp, scale=-1.0), inc="act")
                    hslot = (2 * g + hj) % 4
                    em.wait("vector", "pe", av_m)
                    em.wait("vector", "act", ax)
                    em.wait("vector", "pe", oproj_end_pe)
                    ov = em.op("vector", lambda e, hslot=hslot, n=n, avbank=avbank: e.tensor_tensor(
                        o_att[:, hslot * TH + n * 512:hslot * TH + (n + 1) * 512],
                        s.ps[avbank][:], bc_sb[:], ALU.mult), inc="vec")
                    self.use_bank(avbank, "vec", ov)
                    s.bc_free = ov
                    o_slot_written[(hslot, n)] = ov
            qk_buf_free = last_av
            v_buf_free = last_av

            if g % 2 == 1:
                passi = g // 2
                o_ready = max(o_slot_written.values())
                if DEBUG and half == 0 and g == 1:
                    em.wait("sync", "vec", o_ready)
                    em.op("sync", lambda e, o_att=o_att: e.dma_start(out=s.dbg_oatt, in_=o_att[:]), inc="st", k=16)
                wo_loads = []
                for dk in range(4):
                    em.wait("gpsimd", "pe", oproj_end_pe)
                    ml = em.op("gpsimd", lambda e, passi=passi, dk=dk: e.dma_start(
                        out=wo_sb[:, dk * D:(dk + 1) * D],
                        in_=s.dwout[passi * 4 + dk]), inc="qw", k=16)
                    wo_loads.append(ml)
                last_pe = 0
                em.wait("tensor", "vec", o_ready)
                em.wait("tensor", "qw", wo_loads[-1])
                for tt8 in range(8):
                    banks = [(tt8 * 4 + i) % 6 for i in range(4)] if False else [0, 1, 2, 3]
                    for o in range(4):
                        self.wait_bank(banks[o])
                    mmend = None
                    for dk in range(4):
                        for o in range(4):
                            mmend = em.op("tensor", lambda e, dk=dk, tt8=tt8, o=o, b=banks[o]: e.matmul(
                                s.ps[b][:],
                                o_att[:, dk * TH + tt8 * 128:dk * TH + (tt8 + 1) * 128],
                                wo_sb[:, dk * D + o * 512:dk * D + (o + 1) * 512],
                                start=(dk == 0), stop=(dk == 3)),
                                inc="pe" if (dk == 3 and o == 3) else None)
                    last_pe = mmend
                    em.wait("vector", "pe", mmend)
                    for o in range(4):
                        mv = em.op("vector", lambda e, tt8=tt8, o=o, b=banks[o]: e.tensor_add(
                            s.acc[:, tt8 * D + o * 512:tt8 * D + (o + 1) * 512],
                            s.acc[:, tt8 * D + o * 512:tt8 * D + (o + 1) * 512],
                            s.ps[b][:]), inc="vec")
                        self.use_bank(banks[o], "vec", mv)
                oproj_end_pe = last_pe
                if DEBUG and half == 0 and g == 7:
                    em.wait("sync", "vec", em.cnt["vec"])
                    em.op("sync", lambda e: e.dma_start(out=s.dbg_acc, in_=s.acc[:]), inc="st", k=16)
        s.attn_end_pe = em.cnt["pe"]

    def plan_moe(self, half, a_sc, up_sb, dn_sb, cwb_sb, sg_sb, us_sb):
        s, em = self, self.em
        self.phase_barrier()
        up_slot_free = [0, 0]
        dn_slot_free = [0, 0]
        a_tile_mark = {}
        a_sc_free = 0
        sgus_free = [0, 0]
        cwb_free = 0
        store_marks = []

        for ei in range(NEXP + 2):
            is_shared = ei >= NEXP
            part = ei - NEXP
            if not is_shared:
                em.wait("sync", "vec", s.cw_mark)
                em.wait("sync", "qw", getattr(s, "cw_read_mark", 0))
                m_cwo = em.op("sync", lambda e, ei=ei: e.dma_start(
                    out=s.cw_dram, in_=s.cw_T[ei:ei + 1, :]), inc="qx", k=16)
                em.wait("gpsimd", "qx", m_cwo)
                em.wait("gpsimd", "vec", cwb_free)
                cwb_ready = em.op("gpsimd", lambda e: e.dma_start(
                    out=cwb_sb[:], in_=s.cw_dram.to_broadcast([128, TH])), inc="qw", k=16)
                s.cw_read_mark = cwb_ready

            last_umul = 0
            for mc in range(NIC):
                slot = mc % 2
                em.wait("gpsimd", "pe", up_slot_free[slot])
                src = s.dsup[part, mc] if is_shared else s.dup[ei, mc]
                ml = em.op("gpsimd", lambda e, src=src, slot=slot: e.dma_start(
                    out=up_sb[:, slot * NFC * 256:(slot + 1) * NFC * 256], in_=src), inc="qw", k=16)
                em.wait("tensor", "qw", ml)
                for tt in range(2):
                    pslot = (mc * 2 + tt) % 2
                    gbank, ubank = 2 * pslot, 2 * pslot + 1
                    self.wait_bank(gbank)
                    self.wait_bank(ubank)
                    mg = mu = None
                    for kt in range(NFC):
                        base = slot * NFC * 256 + kt * 256
                        rhs = s.xn_own[:, kt * TH + tt * 512:kt * TH + (tt + 1) * 512]
                        mg = em.op("tensor", lambda e, base=base, rhs=rhs, gbank=gbank, kt=kt: e.matmul(
                            s.ps[gbank][:], up_sb[:, base:base + 128], rhs,
                            start=(kt == 0), stop=(kt == NFC - 1)),
                            inc="pe" if kt == NFC - 1 else None)
                        mu = em.op("tensor", lambda e, base=base, rhs=rhs, ubank=ubank, kt=kt: e.matmul(
                            s.ps[ubank][:], up_sb[:, base + 128:base + 256], rhs,
                            start=(kt == 0), stop=(kt == NFC - 1)),
                            inc="pe" if kt == NFC - 1 else None)
                    up_slot_free[slot] = mu
                    em.wait("scalar", "pe", mg)
                    em.wait("scalar", "vec", sgus_free[pslot])
                    msg = em.op("scalar", lambda e, gbank=gbank, pslot=pslot: e.activation(
                        sg_sb[:, pslot * 512:(pslot + 1) * 512], s.ps[gbank][:], AF.Silu), inc="act")
                    self.use_bank(gbank, "act", msg)
                    em.wait("vector", "pe", mu)
                    em.wait("vector", "vec", sgus_free[pslot])
                    if is_shared:
                        mus = em.op("vector", lambda e, ubank=ubank, pslot=pslot: e.tensor_copy(
                            us_sb[:, pslot * 512:(pslot + 1) * 512], s.ps[ubank][:]), inc="vec")
                    else:
                        em.wait("vector", "qw", cwb_ready)
                        mus = em.op("vector", lambda e, ubank=ubank, pslot=pslot, tt=tt: e.tensor_tensor(
                            us_sb[:, pslot * 512:(pslot + 1) * 512], s.ps[ubank][:],
                            cwb_sb[:, tt * 512:(tt + 1) * 512], ALU.mult), inc="vec")
                    self.use_bank(ubank, "vec", mus)
                    last_umul = mus
                    em.wait("vector", "act", msg)
                    em.wait("vector", "vec", mus)
                    em.wait("vector", "pe", a_sc_free)
                    mam = em.op("vector", lambda e, mc=mc, tt=tt, pslot=pslot: e.tensor_mul(
                        a_sc[:, mc * TH + tt * 512:mc * TH + (tt + 1) * 512],
                        sg_sb[:, pslot * 512:(pslot + 1) * 512],
                        us_sb[:, pslot * 512:(pslot + 1) * 512]), inc="vec")
                    sgus_free[pslot] = mam
                    a_tile_mark[(mc, tt)] = mam
            if not is_shared:
                cwb_free = last_umul

            a_ready = [max(a_tile_mark[(k, tt)] for k in range(NIC)) for tt in range(2)]
            last_down_ldw = 0
            for o in range(4):
                slot = o % 2
                em.wait("gpsimd", "pe", dn_slot_free[slot])
                srcd = s.dsdown[part, o] if is_shared else s.ddown[ei, o]
                mld = em.op("gpsimd", lambda e, srcd=srcd, slot=slot: e.dma_start(
                    out=dn_sb[:, slot * NIC * 512:(slot + 1) * NIC * 512], in_=srcd), inc="qw", k=16)
                em.wait("tensor", "qw", mld)
                for tt8 in range(8):
                    bank = 4 + (tt8 % 2)
                    self.wait_bank(bank)
                    em.wait("tensor", "vec", a_ready[tt8 // 4])
                    md = None
                    for k in range(NIC):
                        md = em.op("tensor", lambda e, k=k, tt8=tt8, slot=slot, o=o, bank=bank: e.matmul(
                            s.ps[bank][:],
                            a_sc[:, k * TH + tt8 * 128:k * TH + (tt8 + 1) * 128],
                            dn_sb[:, slot * NIC * 512 + k * 512:slot * NIC * 512 + (k + 1) * 512],
                            start=(k == 0), stop=(k == NIC - 1)),
                            inc="pe" if k == NIC - 1 else None)
                    dn_slot_free[slot] = md
                    last_down_ldw = md
                    em.wait("vector", "pe", md)
                    mv = em.op("vector", lambda e, tt8=tt8, o=o, bank=bank: e.tensor_add(
                        s.acc[:, tt8 * D + o * 512:tt8 * D + (o + 1) * 512],
                        s.acc[:, tt8 * D + o * 512:tt8 * D + (o + 1) * 512],
                        s.ps[bank][:]), inc="vec")
                    self.use_bank(bank, "vec", mv)
                    if is_shared and part == 1:
                        a_tile_mark[("add", o, tt8)] = mv
            a_sc_free = last_down_ldw

        for tt8 in range(8):
            last = max(a_tile_mark[("add", o, tt8)] for o in range(4))
            em.wait("sync", "vec", last)
            ms = em.op("sync", lambda e, tt8=tt8, half=half: e.dma_start(
                out=s.dy[half * TH + tt8 * 128:half * TH + (tt8 + 1) * 128, :],
                in_=s.acc[:, tt8 * D:(tt8 + 1) * D]), inc="st", k=16)
            store_marks.append(ms)
        s.half0_end_pe = em.cnt["pe"]



def _host_layouts(w_qkv, w_out, gate_w, up_w, down_w, sup_w, sdown_w):
    f32 = np.float32
    wqk = np.empty((NGRP, 4, 128, NFC * 128), f32)
    wq = w_qkv[0:D].reshape(NH, 128, NFC, 128)
    wk = w_qkv[D:2 * D].reshape(NH, 128, NFC, 128)
    for g in range(NGRP):
        for j in range(4):
            src = wq if j < 2 else wk
            h = 2 * g + (j % 2)
            wqk[g, j] = src[h].transpose(2, 1, 0).reshape(128, NFC * 128)
    wv = w_qkv[2 * D:3 * D].reshape(NH, 128, NFC, 128)
    wv_blk = np.empty((NGRP, 128, NFC * 256), f32)
    for g in range(NGRP):
        pair = np.stack([wv[2 * g], wv[2 * g + 1]], axis=0)
        wv_blk[g] = pair.transpose(3, 2, 0, 1).reshape(128, NFC * 256)
    wout_blk = np.ascontiguousarray(
        w_out.reshape(D, NFC, 128).transpose(1, 2, 0))
    gate_blk = np.ascontiguousarray(
        gate_w.reshape(16, NFC, 128).transpose(2, 1, 0).reshape(128, NFC * 16))
    up_blk = np.ascontiguousarray(
        up_w.reshape(NEXP, 2, NIC, 128, NFC, 128)
        .transpose(0, 2, 5, 4, 1, 3)
        .reshape(NEXP, NIC, 128, NFC * 256))
    down_blk = np.ascontiguousarray(
        down_w.reshape(NEXP, 4, 512, NIC, 128)
        .transpose(0, 1, 4, 3, 2)
        .reshape(NEXP, 4, 128, NIC * 512))
    sup_blk = np.ascontiguousarray(
        sup_w.reshape(2, 2, NIC, 128, NFC, 128)
        .transpose(1, 2, 5, 4, 0, 3)
        .reshape(2, NIC, 128, NFC * 256))
    sdown_blk = np.ascontiguousarray(
        sdown_w.reshape(4, 512, 2, NIC, 128)
        .transpose(2, 0, 4, 3, 1)
        .reshape(2, 4, 128, NIC * 512))
    return wqk, wv_blk, wout_blk, gate_blk, up_blk, down_blk, sup_blk, sdown_blk


def _masks():
    ka = np.arange(1024)[:, None]
    qa = np.arange(512)[None, :]
    band = ((qa <= ka) & (ka <= qa + 512)).astype(np.float32)
    m = band.reshape(8, 128, 512).astype(ml_dtypes.bfloat16)
    return m


_BUILT = {}


def _get_program():
    if "nc" not in _BUILT:
        b = Builder()
        _BUILT["nc"] = b.build()
    return _BUILT["nc"]


def kernel(L0_attn_x, L0_attn_w_qkv, L0_attn_w_out, L0_ffn_gate_w,
           L0_ffn_expert_up_w, L0_ffn_expert_down_w,
           L0_ffn_shared_up_w, L0_ffn_shared_down_w):
    x = np.asarray(L0_attn_x, np.float32)
    (wqk, wv_blk, wout_blk, gate_blk, up_blk,
     down_blk, sup_blk, sdown_blk) = _host_layouts(
        np.asarray(L0_attn_w_qkv, np.float32),
        np.asarray(L0_attn_w_out, np.float32),
        np.asarray(L0_ffn_gate_w, np.float32),
        np.asarray(L0_ffn_expert_up_w, np.float32),
        np.asarray(L0_ffn_expert_down_w, np.float32),
        np.asarray(L0_ffn_shared_up_w, np.float32),
        np.asarray(L0_ffn_shared_down_w, np.float32))
    mask_std = _masks()
    mask_first = np.ascontiguousarray(mask_std[0:4])
    mask_first0 = np.zeros_like(mask_first)
    ident32 = np.eye(128, dtype=np.float32)
    ident_bf = np.eye(128, dtype=ml_dtypes.bfloat16)

    shared = {
        "mask_std": mask_std, "ident_f32": ident32, "ident_bf": ident_bf,
        "wqk_blk": wqk, "wv_blk": wv_blk, "wout_blk": wout_blk,
        "gate_blk": gate_blk, "up_blk": up_blk, "down_blk": down_blk,
        "sup_blk": sup_blk, "sdown_blk": sdown_blk,
    }
    in_maps = []
    for c in range(8):
        b, qt = c // 4, c % 4
        rows = x[b, qt * T_OWN:(qt + 1) * T_OWN, :]
        if qt == 0:
            halo = np.zeros((HALO, D), np.float32)
            mf = mask_first0
        else:
            halo = x[b, qt * T_OWN - HALO:qt * T_OWN, :]
            mf = mask_first
        im = dict(shared)
        im["x_own"] = np.ascontiguousarray(rows)
        im["x_halo"] = np.ascontiguousarray(halo)
        im["mask_first"] = mf
        in_maps.append(im)

    nc = _get_program()
    res = run_bass_kernel_spmd(nc, in_maps, core_ids=list(range(8)))
    out = np.empty((2, 8192, D), np.float32)
    for c in range(8):
        b, qt = c // 4, c % 4
        out[b, qt * T_OWN:(qt + 1) * T_OWN, :] = res.results[c]["y"]
    if DEBUG:
        global last_debug
        last_debug = res.results
    return out


last_debug = None


# version 09 (final)
# speedup vs baseline: 1.2824x; 1.0862x over previous
import math
import numpy as np
import ml_dtypes
from contextlib import ExitStack

import concourse.bass as bass
import concourse.mybir as mybir
from concourse.bass_utils import run_bass_kernel_spmd

F32 = mybir.dt.float32
BF16 = mybir.dt.bfloat16
AF = mybir.ActivationFunctionType
ALU = mybir.AluOpType
AX = mybir.AxisListType

D = 2048
NH = 16
HD = 128
NFC = 16
T_OWN = 2048
TH = 1024
HALO = 512
SCALE = 1.0 / math.sqrt(HD)
NEXP = 16
NIC = 11
NGRP = 8

DEBUG = False

ENGINES = ("sync", "gpsimd", "vector", "scalar", "tensor")
SEMS = ("pe", "act", "vec", "pool", "qw", "qx", "st")


class Em:

    def __init__(self):
        self.plan = {e: [] for e in ENGINES}
        self.meta = {e: [] for e in ENGINES}
        self.cnt = {s: 0 for s in SEMS}
        self.wmax = {}
        self.sems = {}

    def emit(self, eng, fn, meta=("o",)):
        self.plan[eng].append(fn)
        self.meta[eng].append(meta)

    def wait(self, eng, sem, val):
        if val is None or val <= 0:
            return
        assert val <= self.cnt[sem], f"wait on future {sem}: {val} > {self.cnt[sem]}"
        key = (eng, sem)
        if self.wmax.get(key, 0) >= val:
            return
        self.wmax[key] = val
        self.emit(eng, lambda e, s=sem, v=val: e.wait_ge(self.sems[s], v), ("w", sem, val))

    def op(self, eng, build, inc=None, k=1):
        if inc is None:
            self.emit(eng, lambda e, b=build: b(e))
            return None
        self.cnt[inc] += k
        m = self.cnt[inc]
        self.emit(eng, lambda e, b=build, s=inc, kk=k: b(e).then_inc(self.sems[s], kk),
                  ("i", inc, k))
        return m

    def simulate(self):
        pos = {e: 0 for e in ENGINES}
        sem = {x: 0 for x in SEMS}
        progress = True
        while progress:
            progress = False
            for e in ENGINES:
                while pos[e] < len(self.meta[e]):
                    m = self.meta[e][pos[e]]
                    if m[0] == "w" and sem[m[1]] < m[2]:
                        break
                    if m[0] == "i":
                        sem[m[1]] += m[2]
                    pos[e] += 1
                    progress = True
        stuck = {e: (pos[e], len(self.meta[e])) for e in ENGINES if pos[e] < len(self.meta[e])}
        if stuck:
            detail = []
            for e, (p, n) in stuck.items():
                detail.append(f"{e}@{p}/{n} blocked on {self.meta[e][p]}")
            raise RuntimeError("schedule deadlock: " + "; ".join(detail))


class Builder:
    def __init__(self):
        self.nc = bass.Bass("TRN2", target_bir_lowering=False, debug=False)
        self.em = Em()

    def build(self):
        nc = self.nc
        em = self.em
        s = self

        s.dx = nc.dram_tensor("x_own", [T_OWN, D], F32, kind="ExternalInput").ap()
        s.dxh = nc.dram_tensor("x_halo", [HALO, D], F32, kind="ExternalInput").ap()
        s.dmask = nc.dram_tensor("mask_std", [8, 128, 512], BF16, kind="ExternalInput").ap()
        s.dmaskf = nc.dram_tensor("mask_first", [4, 128, 512], BF16, kind="ExternalInput").ap()
        s.dident32 = nc.dram_tensor("ident_f32", [128, 128], F32, kind="ExternalInput").ap()
        s.dident_bf = nc.dram_tensor("ident_bf", [128, 128], BF16, kind="ExternalInput").ap()
        s.dwqk = nc.dram_tensor("wqk_blk", [NGRP, 4, 128, NFC * 128], F32, kind="ExternalInput").ap()
        s.dwv = nc.dram_tensor("wv_blk", [NGRP, 128, NFC * 256], F32, kind="ExternalInput").ap()
        s.dwout = nc.dram_tensor("wout_blk", [NFC, 128, D], F32, kind="ExternalInput").ap()
        s.dgate = nc.dram_tensor("gate_blk", [128, NFC * 16], F32, kind="ExternalInput").ap()
        s.dup = nc.dram_tensor("up_blk", [NEXP, NIC, 128, NFC * 256], F32, kind="ExternalInput").ap()
        s.ddown = nc.dram_tensor("down_blk", [NEXP, 4, 128, NIC * 512], F32, kind="ExternalInput").ap()
        s.dsup = nc.dram_tensor("sup_blk", [2, NIC, 128, NFC * 256], F32, kind="ExternalInput").ap()
        s.dsdown = nc.dram_tensor("sdown_blk", [2, 4, 128, NIC * 512], F32, kind="ExternalInput").ap()
        s.dy = nc.dram_tensor("y", [T_OWN, D], F32, kind="ExternalOutput").ap()
        s.rden_dram = nc.dram_tensor("rden_scr", [1, 512], F32, kind="Internal").ap()
        s.cw_dram = nc.dram_tensor("cw_scr", [1, TH], F32, kind="Internal").ap()
        if DEBUG:
            s.dbg_xn = nc.dram_tensor("dbg_xn", [128, NFC * TH], BF16, kind="ExternalOutput").ap()
            s.dbg_xnh = nc.dram_tensor("dbg_xnh", [128, NFC * HALO], BF16, kind="ExternalOutput").ap()
            s.dbg_q = nc.dram_tensor("dbg_q", [128, 2 * TH], BF16, kind="ExternalOutput").ap()
            s.dbg_k = nc.dram_tensor("dbg_k", [128, 2 * (TH + HALO)], BF16, kind="ExternalOutput").ap()
            s.dbg_v = nc.dram_tensor("dbg_v", [128, 12 * 256], BF16, kind="ExternalOutput").ap()
            s.dbg_p = nc.dram_tensor("dbg_p", [128, 8 * 512], BF16, kind="ExternalOutput").ap()
            s.dbg_oatt = nc.dram_tensor("dbg_oatt", [128, 8 * TH], BF16, kind="ExternalOutput").ap()
            s.dbg_acc = nc.dram_tensor("dbg_acc", [128, 8 * D], F32, kind="ExternalOutput").ap()
            s.dbg_cw = nc.dram_tensor("dbg_cw", [16, TH], F32, kind="ExternalOutput").ap()

        with ExitStack() as ctx:
            E = ctx.enter_context
            s.acc = E(nc.sbuf_tensor("acc", [128, 8 * D], F32))
            s.xn_own = E(nc.sbuf_tensor("xn_own", [128, NFC * TH], BF16))
            s.xn_halo = E(nc.sbuf_tensor("xn_halo", [128, NFC * HALO], BF16))
            s.mask_std = E(nc.sbuf_tensor("mask_std_sb", [128, 8 * 512], BF16))
            s.mask_first = E(nc.sbuf_tensor("mask_first_sb", [128, 4 * 512], BF16))
            s.ident32 = E(nc.sbuf_tensor("ident_f32_sb", [128, 128], F32))
            s.ident_bf = E(nc.sbuf_tensor("ident_bf_sb", [128, 128], BF16))
            s.ones_col = E(nc.sbuf_tensor("ones_col", [128, 1], BF16))
            s.ones128 = E(nc.sbuf_tensor("ones128", [128, 128], BF16))
            s.ones_row = E(nc.sbuf_tensor("ones_row", [1, 128], F32))
            s.gate_sb = E(nc.sbuf_tensor("gate_sb", [128, NFC * 16], BF16))
            s.cw_T = E(nc.sbuf_tensor("cw_T", [16, TH], F32))
            s.psA = E(nc.psum_tensor("psA", [128, 1024], F32))
            s.psB = E(nc.psum_tensor("psB", [128, 1024], F32))
            s.psC = E(nc.psum_tensor("psC", [128, 512], F32))
            s.psD = E(nc.psum_tensor("psD", [128, 512], F32))
            s.ps = [s.psA[:, 0:512], s.psA[:, 512:1024],
                    s.psB[:, 0:512], s.psB[:, 512:1024], s.psC[:], s.psD[:]]
            s.ps_pair = [s.psA[:], s.psB[:]]
            s.ptr = [E(nc.psum_tensor(f"ptr{i}", [128, 1024], BF16)) for i in range(2)]
            for nm in SEMS:
                em.sems[nm] = E(nc.semaphore(nm))

            s.bank_free = {i: ("vec", 0) for i in range(6)}
            s.ptr_free = {i: ("vec", 0) for i in range(2)}

            self.plan_init()
            for half in (0, 1):
                self.plan_half(half)
            em.wait("sync", "st", em.cnt["st"])

            self.em.simulate()
            self.replay()
        return nc

    def replay(self):
        nc, em = self.nc, self.em
        with nc.Block() as block:
            for eng_name in ENGINES:
                cmds = em.plan[eng_name]

                def runner(e, cmds=cmds):
                    for c in cmds:
                        c(e)
                getattr(block, eng_name)(runner)

    def phase_barrier(self):
        em = self.em
        pe, act, vec = em.cnt["pe"], em.cnt["act"], em.cnt["vec"]
        for q in ("gpsimd", "sync"):
            em.wait(q, "pe", pe)
            em.wait(q, "act", act)
            em.wait(q, "vec", vec)
        em.wait("vector", "pe", pe)
        em.wait("vector", "act", act)
        em.wait("scalar", "pe", pe)
        em.wait("scalar", "vec", vec)
        if DEBUG:
            st = em.cnt["st"]
            for q in ("gpsimd", "sync", "vector", "scalar", "tensor"):
                em.wait(q, "st", st)

    def wait_bank(self, bank):
        sem, val = self.bank_free[bank]
        self.em.wait("tensor", sem, val)

    def use_bank(self, bank, sem, val):
        self.bank_free[bank] = (sem, val)

    def plan_init(self):
        s, em = self, self.em
        s.m_masks = em.op("sync", lambda e: e.dma_start(
            out=s.mask_std[:], in_=s.dmask.rearrange("j p c -> p j c")), inc="qx", k=16)
        s.m_maskf = em.op("sync", lambda e: e.dma_start(
            out=s.mask_first[:], in_=s.dmaskf.rearrange("j p c -> p j c")), inc="qx", k=16)
        s.m_id32 = em.op("sync", lambda e: e.dma_start(out=s.ident32[:], in_=s.dident32), inc="qx", k=16)
        s.m_id_bf = em.op("sync", lambda e: e.dma_start(out=s.ident_bf[:], in_=s.dident_bf), inc="qx", k=16)
        s.m_gate = em.op("gpsimd", lambda e: e.dma_start(out=s.gate_sb[:], in_=s.dgate), inc="qw", k=16)
        s.m_onescol = em.op("vector", lambda e: e.memset(s.ones_col[:], 1.0), inc="vec")
        s.m_ones128 = em.op("vector", lambda e: e.memset(s.ones128[:], 1.0), inc="vec")
        s.m_ones = em.op("vector", lambda e: e.memset(s.ones_row[:], 1.0), inc="vec")

    def plan_half(self, half):
        s, em, nc = self, self.em, self.nc
        s.half = half

        with ExitStack() as c1:
            E = c1.enter_context
            H = f"_h{half}"
            xh_t = E(nc.sbuf_tensor("xh_t" + H, [128, D], F32))
            lnt = E(nc.sbuf_tensor("lnt" + H, [128, 2 * D], BF16))
            st6 = E(nc.sbuf_tensor("st6" + H, [128, 24], F32))
            st2 = E(nc.sbuf_tensor("st2" + H, [128, 2], F32))
            nmean = E(nc.sbuf_tensor("nmean" + H, [128, 1], F32))
            rstd = E(nc.sbuf_tensor("rstd" + H, [128, 1], F32))
            glg = E(nc.sbuf_tensor("glg" + H, [128, 16], F32))
            gmx = E(nc.sbuf_tensor("gmx" + H, [128, 1], F32))
            gex = E(nc.sbuf_tensor("gex" + H, [128, 16], F32))
            gsum = E(nc.sbuf_tensor("gsum" + H, [128, 1], F32))
            gsc = E(nc.sbuf_tensor("gsc" + H, [128, 16], F32))
            ggs = E(nc.sbuf_tensor("ggs" + H, [128, 8], F32))
            gt8a = E(nc.sbuf_tensor("gt8a" + H, [128, 8], F32))
            gem_ = E(nc.sbuf_tensor("gem" + H, [128, 16], F32))
            gmk = E(nc.sbuf_tensor("gmk" + H, [128, 16], F32))
            gt8b = E(nc.sbuf_tensor("gt8b" + H, [128, 8], F32))
            gw = E(nc.sbuf_tensor("gw" + H, [128, 16], F32))
            gws = E(nc.sbuf_tensor("gws" + H, [128, 1], F32))

            s.ggs_init = em.op("vector", lambda e: e.memset(ggs[:, 4:8], -1e30), inc="vec")

            self.plan_ln(half, xh_t, lnt, st6, st2, nmean, rstd)
            self.plan_gate(half, glg, gmx, gex, gsum, gsc, ggs, gt8a, gem_, gmk, gt8b, gw, gws)

        with ExitStack() as c2:
            A = c2.enter_context
            H = f"_h{half}"
            q_buf = A(nc.sbuf_tensor("q_buf" + H, [128, 2 * TH], BF16))
            k_buf = A(nc.sbuf_tensor("k_buf" + H, [128, 2 * (TH + HALO)], BF16))
            v_buf = A(nc.sbuf_tensor("v_buf" + H, [128, 12 * 256], BF16))
            wqk_sb = A(nc.sbuf_tensor("wqk_sb" + H, [128, 2 * NFC * 128], BF16))
            wv_sb = A(nc.sbuf_tensor("wv_sb" + H, [128, NFC * 256], BF16))
            p_buf = A(nc.sbuf_tensor("p_buf" + H, [128, 8 * 512], BF16))
            o_att = A(nc.sbuf_tensor("o_att" + H, [128, 4 * TH], BF16))
            wo_sb = A(nc.sbuf_tensor("wo_sb" + H, [128, 4 * D], BF16))
            rden = A(nc.sbuf_tensor("rden" + H, [1, 512], F32))
            rden2 = A(nc.sbuf_tensor("rden2" + H, [1, 512], F32))
            bc_sb = A(nc.sbuf_tensor("bc_sb" + H, [128, 512], F32))
            self.plan_attn(half, q_buf, k_buf, v_buf, wqk_sb, wv_sb, p_buf, o_att, wo_sb, rden, rden2, bc_sb)

        with ExitStack() as c3:
            M = c3.enter_context
            H = f"_h{half}"
            a_sc = M(nc.sbuf_tensor("a_sc" + H, [128, NIC * TH], BF16))
            up_sb = M(nc.sbuf_tensor("up_sb" + H, [128, 2 * NFC * 256], BF16))
            dn_sb = M(nc.sbuf_tensor("dn_sb" + H, [128, 2 * NIC * 512], BF16))
            cwb_sb = M(nc.sbuf_tensor("cwb_sb" + H, [128, 2 * 512], F32))
            sg_sb = M(nc.sbuf_tensor("sg_sb" + H, [128, 2 * 512], BF16))
            us_sb = M(nc.sbuf_tensor("us_sb" + H, [128, 2 * 512], BF16))
            self.plan_moe(half, a_sc, up_sb, dn_sb, cwb_sb, sg_sb, us_sb)

    def plan_ln(self, half, xh_t, lnt, st6, st2, nmean, rstd):
        s, em = self, self.em
        self.phase_barrier()
        s.xn_mark = getattr(s, "xn_mark", {})
        ptr_i = 0
        lnt_free = [0, 0]
        lnt_free_sem = ["vec", "vec"]
        xh_free = ("vec", 0)

        if half == 1:
            em.wait("vector", "pe", s.half0_end_pe)
            last = None
            for fc in range(NFC):
                last = em.op("vector", lambda e, fc=fc: e.tensor_copy(
                    s.xn_halo[:, fc * HALO:(fc + 1) * HALO],
                    s.xn_own[:, fc * TH + 512:fc * TH + TH]),
                    inc="vec" if fc == NFC - 1 else None)
            s.halo_copy_mark = last

        tiles = ([("halo", i) for i in range(4)] if half == 0 else []) + \
                [("own", i) for i in range(8)]
        for kind, ti in tiles:
            if kind == "halo":
                em.wait("sync", xh_free[0], xh_free[1])
                mx = em.op("sync", lambda e, ti=ti: e.dma_start(
                    out=xh_t[:], in_=s.dxh[ti * 128:(ti + 1) * 128, :]), inc="qx", k=16)
                xsrc = xh_t[:]
            else:
                mx = em.op("sync", lambda e, ti=ti: e.dma_start(
                    out=s.acc[:, ti * D:(ti + 1) * D],
                    in_=s.dx[half * TH + ti * 128:half * TH + (ti + 1) * 128, :]), inc="qx", k=16)
                xsrc = s.acc[:, ti * D:(ti + 1) * D]

            em.wait("vector", "qx", mx)
            last = None
            for i in range(4):
                last = em.op("vector", lambda e, i=i, xsrc=xsrc: e.bn_stats(
                    st6[:, 6 * i:6 * i + 6], xsrc[:, 512 * i:512 * (i + 1)]),
                    inc="vec" if i == 3 else None)
            em.wait("vector", "vec", last)
            m1 = em.op("vector", lambda e: e.bn_aggr(st2[:], st6[:]), inc="vec")
            em.wait("vector", "vec", m1)
            em.op("vector", lambda e: e.tensor_scalar(nmean[:], st2[:, 0:1], -1.0, None, ALU.mult))
            m2 = em.op("vector", lambda e: e.tensor_scalar(rstd[:], st2[:, 1:2], 1e-5, None, ALU.add), inc="vec")
            em.wait("scalar", "vec", m2)
            a1 = em.op("scalar", lambda e: e.activation(rstd[:], rstd[:], AF.Ln), inc="act")
            em.wait("scalar", "act", a1)
            a2 = em.op("scalar", lambda e: e.activation(rstd[:], rstd[:], AF.Exp, scale=-0.5), inc="act")
            em.wait("vector", "act", a2)
            lslot = (ti + (0 if kind == "own" else 8)) % 2
            em.wait("vector", lnt_free_sem[lslot], lnt_free[lslot])
            m3 = em.op("vector", lambda e, xsrc=xsrc, lslot=lslot: e.tensor_scalar(
                lnt[:, lslot * D:(lslot + 1) * D], xsrc, nmean[:], rstd[:], ALU.add, ALU.mult), inc="vec")

            em.wait("tensor", "vec", m3)
            em.wait("tensor", "qx", s.m_id_bf)
            for b in range(4):
                bank = ptr_i % 2
                ptr_i += 1
                sem, val = s.ptr_free[bank]
                em.wait("tensor", sem, val)
                mt = None
                for j in range(4):
                    fc = 4 * b + j
                    mt = em.op("tensor", lambda e, fc=fc, j=j, bank=bank, lslot=lslot: e.transpose(
                        s.ptr[bank][:, j * 128:(j + 1) * 128],
                        lnt[:, lslot * D + fc * 128:lslot * D + (fc + 1) * 128],
                        s.ident_bf[:]),
                        inc="pe" if j == 3 else None)
                em.wait("scalar", "pe", mt)
                if kind == "own":
                    dst = s.xn_own[:].rearrange("p (f t) -> p f t", t=TH)[:, 4 * b:4 * b + 4, ti * 128:(ti + 1) * 128]
                else:
                    dst = s.xn_halo[:].rearrange("p (f t) -> p f t", t=HALO)[:, 4 * b:4 * b + 4, ti * 128:(ti + 1) * 128]
                mv = em.op("scalar", lambda e, dst=dst, bank=bank: e.copy(
                    dst, s.ptr[bank][:, 0:512].rearrange("p (f t) -> p f t", t=128)), inc="act")
                s.ptr_free[bank] = ("act", mv)
            s.xn_mark[(kind, ti)] = mv
            lnt_free[lslot], lnt_free_sem[lslot] = mt, "pe"
            if kind == "halo":
                xh_free = ("vec", m3)

        s.xn_all_own = max(s.xn_mark[("own", i)] for i in range(8))
        if half == 0:
            s.xn_all_halo = max(s.xn_mark[("halo", i)] for i in range(4))
        else:
            s.xn_all_halo = max(s.halo_copy_mark, 0)
        if DEBUG and half == 0:
            em.wait("sync", "act", max(s.xn_all_own, s.xn_all_halo))
            em.op("sync", lambda e: e.dma_start(out=s.dbg_xn, in_=s.xn_own[:]), inc="st", k=16)
            em.op("sync", lambda e: e.dma_start(out=s.dbg_xnh, in_=s.xn_halo[:]), inc="st", k=16)

    def plan_gate(self, half, glg, gmx, gex, gsum, gsc, ggs, gt8a, gem_, gmk, gt8b, gw, gws):
        s, em = self, self.em
        B5 = 5

        def edge(m):
            em.wait("vector", "vec", m)

        for tt in range(8):
            em.wait("tensor", "act", s.xn_mark[("own", tt)])
            em.wait("tensor", "qw", s.m_gate)
            self.wait_bank(B5)
            mg = None
            for kt in range(NFC):
                mg = em.op("tensor", lambda e, kt=kt, tt=tt: e.matmul(
                    s.ps[B5][:, 0:16],
                    s.xn_own[:, kt * TH + tt * 128:kt * TH + (tt + 1) * 128],
                    s.gate_sb[:, kt * 16:(kt + 1) * 16],
                    start=(kt == 0), stop=(kt == NFC - 1)),
                    inc="pe" if kt == NFC - 1 else None)
            em.wait("scalar", "pe", mg)
            ma = em.op("scalar", lambda e: e.copy(glg[:], s.ps[B5][:, 0:16]), inc="act")
            self.use_bank(B5, "act", ma)

            em.wait("vector", "act", ma)
            v1 = em.op("vector", lambda e: e.reduce_max(gmx[:], glg[:], AX.X, negate=True), inc="vec")
            em.wait("scalar", "vec", v1)
            a2 = em.op("scalar", lambda e: e.activation(gex[:], glg[:], AF.Exp, bias=gmx[:], scale=1.0), inc="act")
            em.wait("vector", "act", a2)
            edge(em.op("vector", lambda e: e.reduce_sum(gsum[:], gex[:], AX.X), inc="vec"))
            edge(em.op("vector", lambda e: e.reciprocal(gsum[:], gsum[:]), inc="vec"))
            edge(em.op("vector", lambda e: e.tensor_scalar(gsc[:], gex[:], gsum[:], None, ALU.mult), inc="vec"))
            em.wait("vector", "vec", s.ggs_init)
            edge(em.op("vector", lambda e: e.tensor_reduce(
                ggs[:, 0:4, None], gsc[:].rearrange("p (g e) -> p g e", e=4), AX.X, ALU.max), inc="vec"))
            edge(em.op("vector", lambda e: e.max(out=gt8a[:], in_=ggs[:]), inc="vec"))
            last = None
            for j in range(4):
                last = em.op("vector", lambda e, j=j: e.tensor_scalar(
                    gem_[:, j::4], ggs[:, 0:4], gt8a[:, 1:2], None, ALU.is_ge),
                    inc="vec" if j == 3 else None)
            edge(last)
            edge(em.op("vector", lambda e: e.tensor_mul(gmk[:], gsc[:], gem_[:]), inc="vec"))
            edge(em.op("vector", lambda e: e.max(out=gt8b[:], in_=gmk[:]), inc="vec"))
            edge(em.op("vector", lambda e: e.tensor_scalar(gw[:], gmk[:], gt8b[:, 3:4], None, ALU.is_ge), inc="vec"))
            edge(em.op("vector", lambda e: e.tensor_mul(gw[:], gw[:], gmk[:]), inc="vec"))
            edge(em.op("vector", lambda e: e.reduce_sum(gws[:], gw[:], AX.X), inc="vec"))
            edge(em.op("vector", lambda e: e.tensor_scalar(gws[:], gws[:], 1e-20, None, ALU.add), inc="vec"))
            edge(em.op("vector", lambda e: e.reciprocal(gws[:], gws[:]), inc="vec"))
            v2 = em.op("vector", lambda e: e.tensor_scalar(gw[:], gw[:], gws[:], None, ALU.mult), inc="vec")

            em.wait("tensor", "vec", v2)
            em.wait("tensor", "qx", s.m_id32)
            self.wait_bank(B5)
            mt = em.op("tensor", lambda e: e.transpose(
                s.ps[B5][0:16, 0:128], gw[:], s.ident32[:]), inc="pe")
            em.wait("vector", "pe", mt)
            mv = em.op("vector", lambda e, tt=tt: e.tensor_copy(
                s.cw_T[:, tt * 128:(tt + 1) * 128], s.ps[B5][0:16, 0:128]), inc="vec")
            self.use_bank(B5, "vec", mv)
        s.cw_mark = mv
        if DEBUG and half == 0:
            em.wait("sync", "vec", mv)
            em.op("sync", lambda e: e.dma_start(out=s.dbg_cw, in_=s.cw_T[:]), inc="st", k=16)

    def plan_attn(self, half, q_buf, k_buf, v_buf, wqk_sb, wv_sb, p_buf, o_att, wo_sb, rden, rden2, bc_sb):
        s, em = self, self.em
        self.phase_barrier()

        wqk_slot_free = [0, 0]
        wv_free = 0
        qk_buf_free = 0
        v_buf_free = 0
        o_slot_written = {}
        oproj_end_pe = 0
        unit_idx = 0

        em.wait("tensor", "act", s.xn_all_own)
        if half == 0:
            em.wait("tensor", "act", s.xn_all_halo)
        else:
            em.wait("tensor", "vec", s.xn_all_halo)

        for g in range(NGRP):
            bank_i = 0
            proj_evacs = []
            for j in range(4):
                slot = j % 2
                em.wait("gpsimd", "pe", wqk_slot_free[slot])
                mload = em.op("gpsimd", lambda e, g=g, j=j, slot=slot: e.dma_start(
                    out=wqk_sb[:, slot * NFC * 128:(slot + 1) * NFC * 128],
                    in_=s.dwqk[g, j]), inc="qw", k=16)
                if j == 0:
                    em.wait("gpsimd", "pe", wv_free)
                    m_wv = em.op("gpsimd", lambda e, g=g: e.dma_start(out=wv_sb[:], in_=s.dwv[g]), inc="qw", k=16)
                hj = j % 2
                is_q = j < 2
                tcs = range(2) if is_q else range(3)
                em.wait("tensor", "qw", mload)
                for tc in tcs:
                    bank = bank_i % 4
                    bank_i += 1
                    self.wait_bank(bank)
                    mm = None
                    for kt in range(NFC):
                        if is_q:
                            rhs = s.xn_own[:, kt * TH + tc * 512:kt * TH + (tc + 1) * 512]
                        elif tc == 0:
                            rhs = s.xn_halo[:, kt * HALO:(kt + 1) * HALO]
                        else:
                            rhs = s.xn_own[:, kt * TH + (tc - 1) * 512:kt * TH + tc * 512]
                        mm = em.op("tensor", lambda e, kt=kt, slot=slot, rhs=rhs, bank=bank: e.matmul(
                            s.ps[bank][:],
                            wqk_sb[:, slot * NFC * 128 + kt * 128:slot * NFC * 128 + (kt + 1) * 128],
                            rhs, start=(kt == 0), stop=(kt == NFC - 1)),
                            inc="pe" if kt == NFC - 1 else None)
                    wqk_slot_free[slot] = mm
                    em.wait("scalar", "pe", mm)
                    em.wait("scalar", "pe", qk_buf_free)
                    if is_q:
                        dst = q_buf[:, hj * TH + tc * 512:hj * TH + (tc + 1) * 512]
                    else:
                        dst = k_buf[:, hj * 1536 + tc * 512:hj * 1536 + (tc + 1) * 512]
                    ma = em.op("scalar", lambda e, dst=dst, bank=bank: e.copy(dst, s.ps[bank][:]), inc="act")
                    self.use_bank(bank, "act", ma)
                    proj_evacs.append(ma)

            em.wait("tensor", "qw", m_wv)
            v_evacs = []
            for tc in range(12):
                bank = bank_i % 4
                bank_i += 1
                self.wait_bank(bank)
                mm = None
                for kt in range(NFC):
                    if tc < 4:
                        lhsT = s.xn_halo[:, kt * HALO + tc * 128:kt * HALO + (tc + 1) * 128]
                    else:
                        lhsT = s.xn_own[:, kt * TH + (tc - 4) * 128:kt * TH + (tc - 3) * 128]
                    mm = em.op("tensor", lambda e, kt=kt, lhsT=lhsT, bank=bank: e.matmul(
                        s.ps[bank][:, 0:256], lhsT, wv_sb[:, kt * 256:(kt + 1) * 256],
                        start=(kt == 0), stop=(kt == NFC - 1)),
                        inc="pe" if kt == NFC - 1 else None)
                wv_free = mm
                em.wait("vector", "pe", mm)
                em.wait("vector", "pe", v_buf_free)
                mv = em.op("vector", lambda e, tc=tc, bank=bank: e.tensor_copy(
                    v_buf[:, tc * 256:(tc + 1) * 256], s.ps[bank][:, 0:256]), inc="vec")
                self.use_bank(bank, "vec", mv)
                v_evacs.append(mv)

            proj_done_act = max(proj_evacs)
            v_done_vec = max(v_evacs)
            if DEBUG and half == 0 and g == 0:
                em.wait("sync", "act", proj_done_act)
                em.wait("sync", "vec", v_done_vec)
                em.op("sync", lambda e, q_buf=q_buf: e.dma_start(out=s.dbg_q, in_=q_buf[:]), inc="st", k=16)
                em.op("sync", lambda e, k_buf=k_buf: e.dma_start(out=s.dbg_k, in_=k_buf[:]), inc="st", k=16)
                em.op("sync", lambda e, v_buf=v_buf: e.dma_start(out=s.dbg_v, in_=v_buf[:]), inc="st", k=16)

            last_av = 0
            for hj in range(2):
                for n in range(2):
                    unit_idx += 1
                    em.wait("tensor", "act", proj_done_act)
                    em.wait("tensor", "vec", v_done_vec)
                    mask_marks = []
                    den_m = av_m = None
                    avbank, denbank = 4, 5

                    def em_spair(jp):
                        pr = jp % 2
                        self.wait_bank(2 * pr)
                        self.wait_bank(2 * pr + 1)
                        sm = None
                        for h_ in range(2):
                            jt = 2 * jp + h_
                            sm = em.op("tensor", lambda e, hj=hj, n=n, jt=jt, b=2 * pr + h_: e.matmul(
                                s.ps[b][:],
                                k_buf[:, hj * 1536 + n * 512 + jt * 128:hj * 1536 + n * 512 + (jt + 1) * 128],
                                q_buf[:, hj * TH + n * 512:hj * TH + (n + 1) * 512],
                                start=True, stop=True),
                                inc="pe" if h_ == 1 else None)
                        em.wait("scalar", "pe", sm)
                        ea = em.op("scalar", lambda e, jp=jp, pr=pr: e.activation(
                            p_buf[:, jp * 1024:(jp + 1) * 1024], s.ps_pair[pr], AF.Exp, scale=SCALE), inc="act")
                        self.use_bank(2 * pr, "act", ea)
                        self.use_bank(2 * pr + 1, "act", ea)
                        em.wait("vector", "act", ea)
                        em.wait("vector", "qx", s.m_maskf)
                        if half == 0 and n == 0 and jp < 2:
                            mask_ap = s.mask_first[:, jp * 1024:(jp + 1) * 1024]
                        else:
                            mask_ap = s.mask_std[:, jp * 1024:(jp + 1) * 1024]
                        mk = em.op("vector", lambda e, jp=jp, mask_ap=mask_ap: e.tensor_mul(
                            p_buf[:, jp * 1024:(jp + 1) * 1024],
                            p_buf[:, jp * 1024:(jp + 1) * 1024], mask_ap), inc="vec")
                        mask_marks.append(mk)

                    def em_dapair(jp):
                        nonlocal den_m, av_m
                        em.wait("tensor", "vec", mask_marks[jp])
                        if jp == 0:
                            self.wait_bank(avbank)
                            self.wait_bank(denbank)
                            em.wait("tensor", "vec", s.m_ones128)
                        for h_ in range(2):
                            jt = 2 * jp + h_
                            den_m = em.op("tensor", lambda e, jt=jt: e.matmul(
                                s.ps[denbank][:], s.ones128[:], p_buf[:, jt * 512:(jt + 1) * 512],
                                start=(jt == 0), stop=(jt == 7)),
                                inc="pe" if jt == 7 else None)
                            av_m = em.op("tensor", lambda e, hj=hj, n=n, jt=jt: e.matmul(
                                s.ps[avbank][:], v_buf[:, (n * 4 + jt) * 256 + hj * 128:(n * 4 + jt) * 256 + (hj + 1) * 128],
                                p_buf[:, jt * 512:(jt + 1) * 512],
                                start=(jt == 0), stop=(jt == 7)),
                                inc="pe" if jt == 7 else None)

                    for jp in range(4):
                        em_spair(jp)
                        if jp >= 1:
                            em_dapair(jp - 1)
                    em_dapair(3)
                    last_av = av_m
                    if DEBUG and half == 0 and g == 0 and hj == 0 and n == 0:
                        em.wait("sync", "vec", max(mask_marks))
                        em.op("sync", lambda e, p_buf=p_buf: e.dma_start(out=s.dbg_p, in_=p_buf[:]), inc="st", k=16)
                    em.wait("scalar", "pe", den_m)
                    em.wait("scalar", "vec", getattr(s, "bc_free", 0))
                    al = em.op("scalar", lambda e: e.activation(bc_sb[:], s.ps[denbank][:], AF.Ln), inc="act")
                    self.use_bank(denbank, "act", al)
                    em.wait("scalar", "act", al)
                    ax = em.op("scalar", lambda e: e.activation(bc_sb[:], bc_sb[:], AF.Exp, scale=-1.0), inc="act")
                    hslot = (2 * g + hj) % 4
                    em.wait("vector", "pe", av_m)
                    em.wait("vector", "act", ax)
                    em.wait("vector", "pe", oproj_end_pe)
                    ov = em.op("vector", lambda e, hslot=hslot, n=n, avbank=avbank: e.tensor_tensor(
                        o_att[:, hslot * TH + n * 512:hslot * TH + (n + 1) * 512],
                        s.ps[avbank][:], bc_sb[:], ALU.mult), inc="vec")
                    self.use_bank(avbank, "vec", ov)
                    s.bc_free = ov
                    o_slot_written[(hslot, n)] = ov
            qk_buf_free = last_av
            v_buf_free = last_av

            if g % 2 == 1:
                passi = g // 2
                o_ready = max(o_slot_written.values())
                if DEBUG and half == 0 and g == 1:
                    em.wait("sync", "vec", o_ready)
                    em.op("sync", lambda e, o_att=o_att: e.dma_start(out=s.dbg_oatt, in_=o_att[:]), inc="st", k=16)
                wo_loads = []
                for dk in range(4):
                    em.wait("gpsimd", "pe", oproj_end_pe)
                    ml = em.op("gpsimd", lambda e, passi=passi, dk=dk: e.dma_start(
                        out=wo_sb[:, dk * D:(dk + 1) * D],
                        in_=s.dwout[passi * 4 + dk]), inc="qw", k=16)
                    wo_loads.append(ml)
                last_pe = 0
                em.wait("tensor", "vec", o_ready)
                em.wait("tensor", "qw", wo_loads[-1])
                for tt8 in range(8):
                    banks = [(tt8 * 4 + i) % 6 for i in range(4)] if False else [0, 1, 2, 3]
                    for o in range(4):
                        self.wait_bank(banks[o])
                    mmend = None
                    for dk in range(4):
                        for o in range(4):
                            mmend = em.op("tensor", lambda e, dk=dk, tt8=tt8, o=o, b=banks[o]: e.matmul(
                                s.ps[b][:],
                                o_att[:, dk * TH + tt8 * 128:dk * TH + (tt8 + 1) * 128],
                                wo_sb[:, dk * D + o * 512:dk * D + (o + 1) * 512],
                                start=(dk == 0), stop=(dk == 3)),
                                inc="pe" if (dk == 3 and o == 3) else None)
                    last_pe = mmend
                    em.wait("vector", "pe", mmend)
                    for o in range(4):
                        mv = em.op("vector", lambda e, tt8=tt8, o=o, b=banks[o]: e.tensor_add(
                            s.acc[:, tt8 * D + o * 512:tt8 * D + (o + 1) * 512],
                            s.acc[:, tt8 * D + o * 512:tt8 * D + (o + 1) * 512],
                            s.ps[b][:]), inc="vec")
                        self.use_bank(banks[o], "vec", mv)
                oproj_end_pe = last_pe
                if DEBUG and half == 0 and g == 7:
                    em.wait("sync", "vec", em.cnt["vec"])
                    em.op("sync", lambda e: e.dma_start(out=s.dbg_acc, in_=s.acc[:]), inc="st", k=16)
        s.attn_end_pe = em.cnt["pe"]

    def plan_moe(self, half, a_sc, up_sb, dn_sb, cwb_sb, sg_sb, us_sb):
        s, em = self, self.em
        self.phase_barrier()
        up_slot_free = [0, 0]
        dn_slot_free = [0, 0]
        a_tile_mark = {}
        a_sc_free = 0
        sgus_free = [0, 0]
        cwb_free = 0
        store_marks = []

        for ei in range(NEXP + 2):
            is_shared = ei >= NEXP
            part = ei - NEXP
            if not is_shared:
                em.wait("sync", "vec", s.cw_mark)
                em.wait("sync", "qw", getattr(s, "cw_read_mark", 0))
                m_cwo = em.op("sync", lambda e, ei=ei: e.dma_start(
                    out=s.cw_dram, in_=s.cw_T[ei:ei + 1, :]), inc="qx", k=16)
                em.wait("gpsimd", "qx", m_cwo)
                em.wait("gpsimd", "vec", cwb_free)
                cwb_ready = em.op("gpsimd", lambda e: e.dma_start(
                    out=cwb_sb[:], in_=s.cw_dram.to_broadcast([128, TH])), inc="qw", k=16)
                s.cw_read_mark = cwb_ready

            last_umul = 0
            for mc in range(NIC):
                slot = mc % 2
                em.wait("gpsimd", "pe", up_slot_free[slot])
                src = s.dsup[part, mc] if is_shared else s.dup[ei, mc]
                ml = em.op("gpsimd", lambda e, src=src, slot=slot: e.dma_start(
                    out=up_sb[:, slot * NFC * 256:(slot + 1) * NFC * 256], in_=src), inc="qw", k=16)
                em.wait("tensor", "qw", ml)
                for tt in range(2):
                    pslot = (mc * 2 + tt) % 2
                    gbank, ubank = 2 * pslot, 2 * pslot + 1
                    self.wait_bank(gbank)
                    self.wait_bank(ubank)
                    mg = mu = None
                    for kt in range(NFC):
                        base = slot * NFC * 256 + kt * 256
                        rhs = s.xn_own[:, kt * TH + tt * 512:kt * TH + (tt + 1) * 512]
                        mg = em.op("tensor", lambda e, base=base, rhs=rhs, gbank=gbank, kt=kt: e.matmul(
                            s.ps[gbank][:], up_sb[:, base:base + 128], rhs,
                            start=(kt == 0), stop=(kt == NFC - 1)),
                            inc="pe" if kt == NFC - 1 else None)
                        mu = em.op("tensor", lambda e, base=base, rhs=rhs, ubank=ubank, kt=kt: e.matmul(
                            s.ps[ubank][:], up_sb[:, base + 128:base + 256], rhs,
                            start=(kt == 0), stop=(kt == NFC - 1)),
                            inc="pe" if kt == NFC - 1 else None)
                    up_slot_free[slot] = mu
                    em.wait("scalar", "pe", mg)
                    em.wait("scalar", "vec", sgus_free[pslot])
                    msg = em.op("scalar", lambda e, gbank=gbank, pslot=pslot: e.activation(
                        sg_sb[:, pslot * 512:(pslot + 1) * 512], s.ps[gbank][:], AF.Silu), inc="act")
                    self.use_bank(gbank, "act", msg)
                    em.wait("vector", "pe", mu)
                    em.wait("vector", "vec", sgus_free[pslot])
                    if is_shared:
                        mus = em.op("vector", lambda e, ubank=ubank, pslot=pslot: e.tensor_copy(
                            us_sb[:, pslot * 512:(pslot + 1) * 512], s.ps[ubank][:]), inc="vec")
                    else:
                        em.wait("vector", "qw", cwb_ready)
                        mus = em.op("vector", lambda e, ubank=ubank, pslot=pslot, tt=tt: e.tensor_tensor(
                            us_sb[:, pslot * 512:(pslot + 1) * 512], s.ps[ubank][:],
                            cwb_sb[:, tt * 512:(tt + 1) * 512], ALU.mult), inc="vec")
                    self.use_bank(ubank, "vec", mus)
                    last_umul = mus
                    em.wait("vector", "act", msg)
                    em.wait("vector", "vec", mus)
                    em.wait("vector", "pe", a_sc_free)
                    mam = em.op("vector", lambda e, mc=mc, tt=tt, pslot=pslot: e.tensor_mul(
                        a_sc[:, mc * TH + tt * 512:mc * TH + (tt + 1) * 512],
                        sg_sb[:, pslot * 512:(pslot + 1) * 512],
                        us_sb[:, pslot * 512:(pslot + 1) * 512]), inc="vec")
                    sgus_free[pslot] = mam
                    a_tile_mark[(mc, tt)] = mam
            if not is_shared:
                cwb_free = last_umul

            a_ready = [max(a_tile_mark[(k, tt)] for k in range(NIC)) for tt in range(2)]
            last_down_ldw = 0
            for o in range(4):
                slot = o % 2
                em.wait("gpsimd", "pe", dn_slot_free[slot])
                srcd = s.dsdown[part, o] if is_shared else s.ddown[ei, o]
                mld = em.op("gpsimd", lambda e, srcd=srcd, slot=slot: e.dma_start(
                    out=dn_sb[:, slot * NIC * 512:(slot + 1) * NIC * 512], in_=srcd), inc="qw", k=16)
                em.wait("tensor", "qw", mld)
                for tt8 in range(8):
                    bank = 4 + (tt8 % 2)
                    self.wait_bank(bank)
                    em.wait("tensor", "vec", a_ready[tt8 // 4])
                    md = None
                    for k in range(NIC):
                        md = em.op("tensor", lambda e, k=k, tt8=tt8, slot=slot, o=o, bank=bank: e.matmul(
                            s.ps[bank][:],
                            a_sc[:, k * TH + tt8 * 128:k * TH + (tt8 + 1) * 128],
                            dn_sb[:, slot * NIC * 512 + k * 512:slot * NIC * 512 + (k + 1) * 512],
                            start=(k == 0), stop=(k == NIC - 1)),
                            inc="pe" if k == NIC - 1 else None)
                    dn_slot_free[slot] = md
                    last_down_ldw = md
                    em.wait("vector", "pe", md)
                    mv = em.op("vector", lambda e, tt8=tt8, o=o, bank=bank: e.tensor_add(
                        s.acc[:, tt8 * D + o * 512:tt8 * D + (o + 1) * 512],
                        s.acc[:, tt8 * D + o * 512:tt8 * D + (o + 1) * 512],
                        s.ps[bank][:]), inc="vec")
                    self.use_bank(bank, "vec", mv)
                    if is_shared and part == 1:
                        a_tile_mark[("add", o, tt8)] = mv
            a_sc_free = last_down_ldw

        for tt8 in range(8):
            last = max(a_tile_mark[("add", o, tt8)] for o in range(4))
            em.wait("sync", "vec", last)
            ms = em.op("sync", lambda e, tt8=tt8, half=half: e.dma_start(
                out=s.dy[half * TH + tt8 * 128:half * TH + (tt8 + 1) * 128, :],
                in_=s.acc[:, tt8 * D:(tt8 + 1) * D]), inc="st", k=16)
            store_marks.append(ms)
        s.half0_end_pe = em.cnt["pe"]



def _host_layouts(w_qkv, w_out, gate_w, up_w, down_w, sup_w, sdown_w):
    f32 = np.float32
    wqk = np.empty((NGRP, 4, 128, NFC * 128), f32)
    wq = w_qkv[0:D].reshape(NH, 128, NFC, 128)
    wk = w_qkv[D:2 * D].reshape(NH, 128, NFC, 128)
    for g in range(NGRP):
        for j in range(4):
            src = wq if j < 2 else wk
            h = 2 * g + (j % 2)
            wqk[g, j] = src[h].transpose(2, 1, 0).reshape(128, NFC * 128)
    wv = w_qkv[2 * D:3 * D].reshape(NH, 128, NFC, 128)
    wv_blk = np.empty((NGRP, 128, NFC * 256), f32)
    for g in range(NGRP):
        pair = np.stack([wv[2 * g], wv[2 * g + 1]], axis=0)
        wv_blk[g] = pair.transpose(3, 2, 0, 1).reshape(128, NFC * 256)
    wout_blk = np.ascontiguousarray(
        w_out.reshape(D, NFC, 128).transpose(1, 2, 0))
    gate_blk = np.ascontiguousarray(
        gate_w.reshape(16, NFC, 128).transpose(2, 1, 0).reshape(128, NFC * 16))
    up_blk = np.ascontiguousarray(
        up_w.reshape(NEXP, 2, NIC, 128, NFC, 128)
        .transpose(0, 2, 5, 4, 1, 3)
        .reshape(NEXP, NIC, 128, NFC * 256))
    down_blk = np.ascontiguousarray(
        down_w.reshape(NEXP, 4, 512, NIC, 128)
        .transpose(0, 1, 4, 3, 2)
        .reshape(NEXP, 4, 128, NIC * 512))
    sup_blk = np.ascontiguousarray(
        sup_w.reshape(2, 2, NIC, 128, NFC, 128)
        .transpose(1, 2, 5, 4, 0, 3)
        .reshape(2, NIC, 128, NFC * 256))
    sdown_blk = np.ascontiguousarray(
        sdown_w.reshape(4, 512, 2, NIC, 128)
        .transpose(2, 0, 4, 3, 1)
        .reshape(2, 4, 128, NIC * 512))
    return wqk, wv_blk, wout_blk, gate_blk, up_blk, down_blk, sup_blk, sdown_blk


def _masks():
    ka = np.arange(1024)[:, None]
    qa = np.arange(512)[None, :]
    band = ((qa <= ka) & (ka <= qa + 512)).astype(np.float32)
    m = band.reshape(8, 128, 512).astype(ml_dtypes.bfloat16)
    return m


_BUILT = {}


def _get_program():
    if "nc" not in _BUILT:
        b = Builder()
        _BUILT["nc"] = b.build()
    return _BUILT["nc"]


def kernel(L0_attn_x, L0_attn_w_qkv, L0_attn_w_out, L0_ffn_gate_w,
           L0_ffn_expert_up_w, L0_ffn_expert_down_w,
           L0_ffn_shared_up_w, L0_ffn_shared_down_w):
    x = np.asarray(L0_attn_x, np.float32)
    (wqk, wv_blk, wout_blk, gate_blk, up_blk,
     down_blk, sup_blk, sdown_blk) = _host_layouts(
        np.asarray(L0_attn_w_qkv, np.float32),
        np.asarray(L0_attn_w_out, np.float32),
        np.asarray(L0_ffn_gate_w, np.float32),
        np.asarray(L0_ffn_expert_up_w, np.float32),
        np.asarray(L0_ffn_expert_down_w, np.float32),
        np.asarray(L0_ffn_shared_up_w, np.float32),
        np.asarray(L0_ffn_shared_down_w, np.float32))
    mask_std = _masks()
    mask_first = np.ascontiguousarray(mask_std[0:4])
    mask_first0 = np.zeros_like(mask_first)
    ident32 = np.eye(128, dtype=np.float32)
    ident_bf = np.eye(128, dtype=ml_dtypes.bfloat16)

    shared = {
        "mask_std": mask_std, "ident_f32": ident32, "ident_bf": ident_bf,
        "wqk_blk": wqk, "wv_blk": wv_blk, "wout_blk": wout_blk,
        "gate_blk": gate_blk, "up_blk": up_blk, "down_blk": down_blk,
        "sup_blk": sup_blk, "sdown_blk": sdown_blk,
    }
    in_maps = []
    for c in range(8):
        b, qt = c // 4, c % 4
        rows = x[b, qt * T_OWN:(qt + 1) * T_OWN, :]
        if qt == 0:
            halo = np.zeros((HALO, D), np.float32)
            mf = mask_first0
        else:
            halo = x[b, qt * T_OWN - HALO:qt * T_OWN, :]
            mf = mask_first
        im = dict(shared)
        im["x_own"] = np.ascontiguousarray(rows)
        im["x_halo"] = np.ascontiguousarray(halo)
        im["mask_first"] = mf
        in_maps.append(im)

    nc = _get_program()
    res = run_bass_kernel_spmd(nc, in_maps, core_ids=list(range(8)))
    out = np.empty((2, 8192, D), np.float32)
    for c in range(8):
        b, qt = c // 4, c % 4
        out[b, qt * T_OWN:(qt + 1) * T_OWN, :] = res.results[c]["y"]
    if DEBUG:
        global last_debug
        last_debug = res.results
    return out


last_debug = None
